# version 08
# baseline (speedup 1.0000x reference)
import numpy as np
import concourse.bass as bass
import concourse.mybir as mybir
from concourse.bass_utils import run_bass_kernel_spmd
from concourse.alu_op_type import AluOpType as OP

F32 = mybir.dt.float32
BF16 = mybir.dt.bfloat16
U16 = mybir.dt.uint16
I32 = mybir.dt.int32
AF = mybir.ActivationFunctionType
AX = mybir.AxisListType.X

S = 4096
D = 4096
NB = 2
CPB = 4
NQT = 8
NOWN = NQT * 128
NKVH = 8
NQH = 32
HD = 128
DC = D // 128
NEG = -1.0e30
EPS = 1e-5
TWO_PI = float(2.0 * np.pi)
PI = float(np.pi)


class Sem:
    def __init__(self, handle):
        self.h = handle
        self.n = 0

    def inc(self, instr, k=1):
        instr.then_inc(self.h, k)
        self.n += k
        return instr

    def wait(self, eng, val=None):
        v = self.n if val is None else val
        if v:
            eng.wait_ge(self.h, v)


def build_program(dbg=(), dbg_s4=False):
    nc = bass.Bass("TRN2", target_bir_lowering=False)

    def din(name, shape, dt=F32):
        return nc.dram_tensor(name, shape, dt, kind="ExternalInput").ap()

    def dsc(name, shape, dt=F32):
        kind = "ExternalOutput" if name in dbg else "Internal"
        return nc.dram_tensor(name, shape, dt, kind=kind).ap()

    xcov = din("xcov", [D, S])
    xown = din("xown", [D, NOWN])
    wq_i = din("wq", [D, D])
    wk_i = din("wk", [D, NKVH * HD])
    wv_i = din("wv", [D, NKVH * HD])
    wo_i = din("wo", [D, D])
    h4_i = din("h4", [4, D, D], U16)
    lut4_i = din("lut4", [D, 16])
    bias_i = din("bias0", [128, DC])
    posq_i = din("posq", [128, NOWN])
    mqm_i = din("mqm", [NQT, 4, 128, 512], BF16)
    lam_re = din("lam_re", [1, 64]); lam_im = din("lam_im", [1, 64])
    p_re = din("p_re", [1, 64]); p_im = din("p_im", [1, 64])
    q_re = din("q_re", [1, 64]); q_im = din("q_im", [1, 64])
    ct_re = din("ct_re", [1, 64]); ct_im = din("ct_im", [1, 64])
    vc_reT = din("vc_reT", [64, 64]); vc_imT = din("vc_imT", [64, 64])
    bp_i = din("bp", [64, 1])
    dd_i = din("dsc_d", [1, 1])
    ls_i = din("lstep", [1, 1])
    ar_i = din("a_r", [D, 1024])
    br_i = din("b_r", [1024, D])
    wr_i = din("wr_r", [D, 64])
    e8_i = din("e8", [64, 1024])

    yout = nc.dram_tensor("yout", [D, NOWN], F32, kind="ExternalOutput").ap()

    xnch = dsc("xnch", [D, S], BF16); xncl = dsc("xncl", [D, S], BF16)
    xnoh = dsc("xnoh", [D, NOWN], BF16); xnol = dsc("xnol", [D, NOWN], BF16)
    kdh = dsc("kdh", [NKVH, HD, S], BF16); kdl = dsc("kdl", [NKVH, HD, S], BF16)
    vdh = dsc("vdh", [NKVH, S, HD], BF16); vdl = dsc("vdl", [NKVH, S, HD], BF16)
    qdh = dsc("qdh", [NKVH, HD, 4, NOWN], BF16)
    qdl = dsc("qdl", [NKVH, HD, 4, NOWN], BF16)
    wqh = dsc("wqh", [D, D], BF16); wql = dsc("wql", [D, D], BF16)
    wkh = dsc("wkh", [D, NKVH * HD], BF16); wkl = dsc("wkl", [D, NKVH * HD], BF16)
    wvh = dsc("wvh", [D, NKVH * HD], BF16); wvl = dsc("wvl", [D, NKVH * HD], BF16)
    woh = dsc("woh", [D, D], BF16); wol = dsc("wol", [D, D], BF16)
    arh = dsc("arh", [D, 1024], BF16); arl = dsc("arl", [D, 1024], BF16)
    brh = dsc("brh", [1024, D], BF16); brl = dsc("brl", [1024, D], BF16)
    wlh = dsc("wlh", [D, D], BF16); wll = dsc("wll", [D, D], BF16)
    wrdh = dsc("wrdh", [D, 64], BF16); wrdl = dsc("wrdl", [D, 64], BF16)
    yffn = dsc("yffn", [D, NOWN])
    aodh = dsc("aodh", [D, NOWN], BF16); aodl = dsc("aodl", [D, NOWN], BF16)
    x1d = dsc("x1d", [D, NOWN])
    x1nh = dsc("x1nh", [D, NOWN], BF16); x1nl = dsc("x1nl", [D, NOWN], BF16)
    kpad = dsc("kpad", [1, 4224])
    dsum = dsc("dsum", [1, S]); dsq = dsc("dsq", [1, S])
    if dbg_s4:
        s4dbg = nc.dram_tensor("s4dbg", [12, 128, 32], F32, kind="ExternalOutput").ap()
        s4dbg2 = nc.dram_tensor("s4dbg2", [4, 128, 64], F32, kind="ExternalOutput").ap()
        s4dbg3 = nc.dram_tensor("s4dbg3", [8, 128, 128], F32, kind="ExternalOutput").ap()

    ctxs = []

    def sb(name, shape, dt=F32):
        t = nc.sbuf_tensor(name, shape, dt)
        h = t.__enter__()
        ctxs.append(t)
        return h

    def psum(name, shape, dt=F32):
        t = nc.psum_tensor(name, shape, dt)
        h = t.__enter__()
        ctxs.append(t)
        return h

    def mksem(name):
        t = nc.semaphore(name)
        h = t.__enter__()
        ctxs.append(t)
        return Sem(h)

    identb = sb("identb", [128, 128], BF16)
    identf = sb("identf", [128, 128], F32)
    revf = sb("revf", [128, 128], F32)
    idint = sb("idint", [128, 128], I32)
    idint2 = sb("idint2", [128, 128], I32)
    ones_col_b = sb("ones_col_b", [128, 1], BF16)
    ones_row_f = sb("ones_row_f", [1, 128], F32)
    ones_col_f = sb("ones_col_f", [128, 1], F32)
    invf = sb("invf", [128, 1])
    pcol = sb("pcol", [128, 1])
    csign = sb("csign", [128, 1])
    cpi = sb("cpi", [128, 1])
    ceps = sb("ceps", [128, 1])
    gxf = sb("gxf", [128, 8 * 512])
    cq = gxf[:, 0:NOWN]; sq = gxf[:, NOWN:2 * NOWN]
    posqs = gxf[:, 2 * NOWN:3 * NOWN]
    sra = sb("sra", [1, 512]); srb = sb("srb", [1, 512])
    xa = sb("xa", [128, 512]); xb = sb("xb", [128, 512])
    tu = sb("tu", [128, 512]); tv = sb("tv", [128, 512])
    thA = sb("thA", [128, 512], BF16); tlA = sb("tlA", [128, 512], BF16)
    thB = sb("thB", [128, 512], BF16); tlB = sb("tlB", [128, 512], BF16)
    xph = sb("xph", [128, DC * 512], BF16)
    xpl = sb("xpl", [128, DC * 512], BF16)
    wph = sb("wph", [128, DC * 128], BF16)
    wpl = sb("wpl", [128, DC * 128], BF16)
    rint = sb("rint", [128, 512], I32)
    bcast_a = sb("bcast_a", [128, 512]); bcast_b = sb("bcast_b", [128, 512])
    tw = bcast_a
    posk = bcast_b
    rsw = bcast_a
    qth = sb("qth", [128, 512], BF16); qtl = sb("qtl", [128, 512], BF16)
    kthA = sb("kthA", [128, 128], BF16); ktlA = sb("ktlA", [128, 128], BF16)
    kthB = sb("kthB", [128, 128], BF16); ktlB = sb("ktlB", [128, 128], BF16)
    vthA = sb("vthA", [128, 128], BF16); vtlA = sb("vtlA", [128, 128], BF16)
    vthB = sb("vthB", [128, 128], BF16); vtlB = sb("vtlB", [128, 128], BF16)
    prf = sb("prf", [128, 512])
    ck = prf
    gt64 = prf[0:64, :]
    prhA = sb("prhA", [128, 512], BF16); prlA = sb("prlA", [128, 512], BF16)
    prhB = sb("prhB", [128, 512], BF16); prlB = sb("prlB", [128, 512], BF16)
    mq4 = sb("mq4", [128, 4 * 512], BF16)
    dnv = sra
    _unused_dnv = None
    dnb = sb("dnb", [128, 512])
    sk = dnb
    bigph = sb("bigph", [128, DC * 128], BF16)
    bigpl = sb("bigpl", [128, DC * 128], BF16)
    aoph, aopl = bigph, bigpl
    bcol = sb("bcol", [128, DC])
    h4t = sb("h4t", [128, 4 * 1024], U16)
    jt16 = sb("jt16", [128, 1024], U16)
    sh16 = sb("sh16", [128, 1024], U16)
    ix16 = sb("ix16", [128, 1024], U16)
    eqt = sb("eqt", [128, 1024], BF16)
    dwh, dwl = bigph, bigpl
    luth = sb("luth", [128, 16])
    lutl = sb("lutl", [128, 16])
    lutf = sb("lutf", [128, 16])
    bdh, bdl = bigph, bigpl
    bdf = sb("bdf", [128, 128])
    hgh = sb("hgh", [128, 8 * 512], BF16)
    hgl = sb("hgl", [128, 8 * 512], BF16)
    e8f = sb("e8f", [64, 1024])
    sc64 = sb("sc64", [128, 64]); sc64b = sb("sc64b", [128, 64])
    msk64 = sb("msk64", [128, 64]); g64 = sb("g64", [128, 64])
    dn64 = sb("dn64", [128, 1])
    m8a = sb("m8a", [128, 8]); m8b = sb("m8b", [128, 8])
    omre = sb("omre", [128, 32]); omim = sb("omim", [128, 32])
    gre = sb("gre", [128, 32]); gim = sb("gim", [128, 32])
    cre = sb("cre", [128, 32]); cim = sb("cim", [128, 32])
    k00r = sb("k00r", [128, 32]); k00i = sb("k00i", [128, 32])
    k01r = sb("k01r", [128, 32]); k01i = sb("k01i", [128, 32])
    k10r = sb("k10r", [128, 32]); k10i = sb("k10i", [128, 32])
    k11r = sb("k11r", [128, 32]); k11i = sb("k11i", [128, 32])
    arr = sb("arr", [128, 32]); ari = sb("ari", [128, 32]); ariN = sb("ariN", [128, 32])
    t32a = sb("t32a", [128, 32]); t32b = sb("t32b", [128, 32]); t32c = sb("t32c", [128, 32])
    t32d = sb("t32d", [128, 32])
    lrb = sb("lrb", [128, 64]); lib = sb("lib", [128, 64])
    n1r = sb("n1r", [128, 64]); n1i = sb("n1i", [128, 64])
    n2r = sb("n2r", [128, 64]); n2i = sb("n2i", [128, 64])
    n3r = sb("n3r", [128, 64]); n3i = sb("n3i", [128, 64])
    n4r = sb("n4r", [128, 64]); n4i = sb("n4i", [128, 64])
    t64a = sb("t64a", [128, 64]); t64b = sb("t64b", [128, 64]); t64c = sb("t64c", [128, 64])
    t64d = sb("t64d", [128, 64]); t64e = sb("t64e", [128, 64])
    r1 = sb("r1", [1, 64]); r2 = sb("r2", [1, 64]); r3 = sb("r3", [1, 64]); r4 = sb("r4", [1, 64])
    r5 = sb("r5", [1, 64]); r6 = sb("r6", [1, 64]); r7 = sb("r7", [1, 64]); r8 = sb("r8", [1, 64])
    tr9 = sb("tr9", [1, 64])
    sm64 = [sb(f"sm64_{i}", [1, 64]) for i in range(8)]
    brea = sb("brea", [1, 64]); bima = sb("bima", [1, 64])
    wc128 = sb("wc128", [128, 128]); ws128 = sb("ws128", [128, 128])
    twc = sb("twc", [128, 32]); tws = sb("tws", [128, 32])
    c32s = sb("c32s", [32, 32]); s32sn = sb("s32sn", [32, 32])
    yre = sb("yre", [128, 32]); yim = sb("yim", [128, 32])
    yreT = sb("yreT", [32, 128]); yimT = sb("yimT", [32, 128])
    ktap = sb("ktap", [32, 128])
    stepc = sb("stepc", [128, 1]); dcol = sb("dcol", [128, 1])
    zrow = sb("zrow", [1, 128])
    s4v = sb("s4v", [64, 64]); s4w = sb("s4w", [64, 64]); s4p = sb("s4p", [64, 1])

    ps0 = psum("ps0", [128, 512]); ps1 = psum("ps1", [128, 512])
    ps2 = psum("ps2", [128, 512]); ps3 = psum("ps3", [128, 512])
    ps4 = psum("ps4", [128, 512]); ps5 = psum("ps5", [128, 512])
    pbt = psum("pbt", [128, 128], BF16)
    pft = psum("pft", [128, 128], F32)

    sd = mksem("sd")
    sd2 = mksem("sd2")
    sdA = mksem("sdA")
    sdB = mksem("sdB")
    sdq = mksem("sdq")
    sdm = mksem("sdm")
    sdl = mksem("sdl")
    s2A = mksem("s2A")
    s2B = mksem("s2B")
    sdp = mksem("sdp")
    sdw = mksem("sdw")
    sv = mksem("sv")
    stm = mksem("stm")
    ssm = mksem("ssm")
    sgm = mksem("sgm")
    sout = mksem("sout")

    TE, VE, SE, GE, SY = nc.tensor, nc.vector, nc.scalar, nc.gpsimd, nc.sync

    def mark(eng, s):
        eng.drain()
        eng.sem_inc(s.h, 1)
        s.n += 1
        return s.n

    def barrier():
        mark(TE, stm); mark(VE, sv); mark(SE, ssm); mark(GE, sgm)
        for eng in (TE, VE, SE, GE, SY):
            stm.wait(eng); sv.wait(eng); ssm.wait(eng); sgm.wait(eng)
            sd.wait(eng); sd2.wait(eng); sdA.wait(eng); sdB.wait(eng)
            sdq.wait(eng); sdm.wait(eng); sdl.wait(eng)
            s2A.wait(eng); s2B.wait(eng); sdp.wait(eng); sdw.wait(eng)

    def vtt(out, a, b, op):
        VE.tensor_tensor(out, a, b, op)
        VE.drain()

    def vts(out, a, s1, s2, op1, op2=...):
        if op2 is ...:
            VE.tensor_scalar(out, a, s1, None, op1)
        else:
            VE.tensor_scalar(out, a, s1, s2, op1, op2)
        VE.drain()

    def vstt(out, in0, scal, in1, op0, op1):
        VE.scalar_tensor_tensor(out, in0, scal, in1, op0, op1)
        VE.drain()

    def vcp(out, a):
        VE.tensor_copy(out, a)
        VE.drain()

    def sincol(dst, npart=128):
        SE.activation(dst, dst, AF.Sin, bias=0.0, scale=1.0)

    def rrange(buf, ncols, npart=128):
        for c0 in range(0, ncols, 512):
            w = min(512, ncols - c0)
            ch = buf[0:npart, c0:c0 + w]
            vts(tu[0:npart, 0:w], ch, float(1.0 / TWO_PI), None, OP.mult)
            vcp(rint[0:npart, 0:w], tu[0:npart, 0:w])
            vcp(tu[0:npart, 0:w], rint[0:npart, 0:w])
            vstt(ch, tu[0:npart, 0:w], -TWO_PI, ch, OP.mult, OP.add)
            vts(ch, ch, PI, None, OP.min)
            vts(ch, ch, -PI, None, OP.max)

    with nc.Block():
        GE.iota(idint[:], [[1, 128]], base=0, channel_multiplier=-1)
        GE.iota(idint2[:], [[1, 128]], base=0, channel_multiplier=1)
        GE.iota(pcol[:], [[1, 1]], base=0, channel_multiplier=1,
                allow_small_or_imprecise_dtypes=True)
        GE.iota(jt16[:].rearrange("p (h j) -> p h j", j=16), [[0, 64], [1, 16]],
                base=0, channel_multiplier=0, allow_small_or_imprecise_dtypes=True)
        GE.memset(ones_col_b[:], 1.0)
        GE.memset(ones_row_f[:], 1.0)
        GE.memset(ones_col_f[:], 1.0)
        GE.memset(cpi[:], PI)
        GE.memset(ceps[:], EPS)
        GE.memset(csign[0:64, :], -1.0)
        GE.memset(csign[64:128, :], 1.0)
        GE.memset(zrow[:], 0.0)
        mark(GE, sgm)
        sgm.wait(VE)
        vts(identb[:], idint[:], 0, None, OP.is_equal)
        vts(identf[:], idint[:], 0, None, OP.is_equal)
        vts(revf[:], idint2[:], 127.0, None, OP.is_equal)
        vts(invf[:], pcol[:], 64.0, None, OP.is_ge)
        vstt(invf[:], invf[:], -64.0, pcol[:], OP.mult, OP.add)
        mark(VE, sv)
        sv.wait(SE)
        SE.activation(invf[:], invf[:], AF.Exp, bias=0.0,
                      scale=float(-np.log(10000.0) / 64.0))
        mark(SE, ssm)
        sd.inc(SY.dma_start(posqs[:], posq_i), 16)
        sd.inc(SY.dma_start(bcol[:], bias_i), 16)
        sd.inc(SY.dma_start(e8f[:], e8_i), 16)
        sd.wait(VE)
        ssm.wait(VE)
        for half, dst in ((0, cq), (1, sq)):
            vts(dst[:], posqs[:], invf[:, 0:1], None, OP.mult)
            if half == 0:
                vts(dst[:], dst[:], PI / 2, None, OP.add)
            rrange(dst, NOWN)
        mark(VE, sv)
        sv.wait(SE)
        sincol(cq[:]); sincol(sq[:])
        mark(SE, ssm)
        ssm.wait(VE)
        vts(cq[:], cq[:], float(1.0 / np.sqrt(HD)), None, OP.mult)
        vts(sq[:], sq[:], csign[:, 0:1], float(1.0 / np.sqrt(HD)), OP.mult, OP.mult)
        barrier()

        consumed = {0: 0, 1: 0}
        stored = {0: 0, 1: 0}

        def stream_split(src, dsth, dstl, rows, cols):
            for r in range(rows // 128):
                for q in range(cols // 512):
                    i = r * (cols // 512) + q
                    bi = i % 2
                    buf = xa if bi == 0 else xb
                    th = thA if bi == 0 else thB
                    tl = tlA if bi == 0 else tlB
                    sdx = sdA if bi == 0 else sdB
                    s2x = s2A if bi == 0 else s2B
                    sv.wait(SY, consumed[bi])
                    sdx.inc(SY.dma_start(buf[:], src[128 * r:128 * (r + 1), 512 * q:512 * (q + 1)]), 16)
                    sdx.wait(VE)
                    s2x.wait(VE, stored[bi])
                    vcp(th[:], buf[:])
                    vtt(tl[:], buf[:], th[:], OP.subtract)
                    consumed[bi] = mark(VE, sv)
                    sv.wait(SE)
                    s2x.inc(SE.dma_start(dsth[128 * r:128 * (r + 1), 512 * q:512 * (q + 1)], th[:]), 16)
                    s2x.inc(SE.dma_start(dstl[128 * r:128 * (r + 1), 512 * q:512 * (q + 1)], tl[:]), 16)
                    stored[bi] = s2x.n

        stream_split(wq_i, wqh, wql, D, D)
        stream_split(wk_i, wkh, wkl, D, NKVH * HD)
        stream_split(wv_i, wvh, wvl, D, NKVH * HD)
        stream_split(wo_i, woh, wol, D, D)
        stream_split(ar_i, arh, arl, D, 1024)
        stream_split(br_i, brh, brl, 1024, D)
        for ic in range(DC):
            bi = ic % 2
            buf = xa if bi == 0 else xb
            th = thA if bi == 0 else thB
            tl = tlA if bi == 0 else tlB
            sdx = sdA if bi == 0 else sdB
            s2x = s2A if bi == 0 else s2B
            sv.wait(SY, consumed[bi])
            sdx.inc(SY.dma_start(buf[:, 0:64], wr_i[128 * ic:128 * (ic + 1), :]), 16)
            sdx.wait(VE)
            s2x.wait(VE, stored[bi])
            vcp(th[:, 0:64], buf[:, 0:64])
            vtt(tl[:, 0:64], buf[:, 0:64], th[:, 0:64], OP.subtract)
            consumed[bi] = mark(VE, sv)
            sv.wait(SE)
            s2x.inc(SE.dma_start(wrdh[128 * ic:128 * (ic + 1), :], th[:, 0:64]), 16)
            s2x.inc(SE.dma_start(wrdl[128 * ic:128 * (ic + 1), :], tl[:, 0:64]), 16)
            stored[bi] = s2x.n
        barrier()

        def stats_sweep(src, ncols, psA, psB, sum_dst, sq_dst):
            se_done = {0: 0, 1: 0}
            te_done = {0: 0, 1: 0}
            for nt in range(ncols // 512):
                for ic in range(DC):
                    bi = ic % 2
                    buf = xa if bi == 0 else xb
                    sqb = tu if bi == 0 else tv
                    sdx = sdA if bi == 0 else sdB
                    stm.wait(SY, te_done[bi])
                    ssm.wait(SY, se_done[bi])
                    sdx.inc(SY.dma_start(buf[:], src[128 * ic:128 * (ic + 1), 512 * nt:512 * (nt + 1)]), 16)
                    sdx.wait(SE)
                    stm.wait(SE, te_done[bi])
                    SE.activation(sqb[:], buf[:], AF.Square)
                    se_done[bi] = mark(SE, ssm)
                    sdx.wait(TE)
                    stm.inc(TE.matmul(psA[0:1, :], ones_col_f[:], buf[:], start=(ic == 0), stop=(ic == DC - 1)))
                    ssm.wait(TE, se_done[bi])
                    stm.inc(TE.matmul(psB[0:1, :], ones_col_f[:], sqb[:], start=(ic == 0), stop=(ic == DC - 1)))
                    te_done[bi] = stm.n
                stm.wait(VE)
                sd2.wait(VE)
                vcp(sra[:], psA[0:1, :])
                vcp(srb[:], psB[0:1, :])
                mark(VE, sv)
                sv.wait(TE)
                sv.wait(SE)
                sd2.inc(SE.dma_start(sum_dst[:, 512 * nt:512 * (nt + 1)], sra[:]), 16)
                sd2.inc(SE.dma_start(sq_dst[:, 512 * nt:512 * (nt + 1)], srb[:]), 16)

        def ln_finish(n):
            for c0 in range(0, n, 512):
                sv.wait(SY)
                sd2.wait(SY)
                sd.inc(SY.dma_start(sra[:], dsum[:, c0:c0 + 512]), 16)
                sd.inc(SY.dma_start(srb[:], dsq[:, c0:c0 + 512]), 16)
                sd.wait(VE)
                sd2.wait(VE)
                c_ = tu[0:1, :]
                vts(sra[:], sra[:], 1.0 / D, None, OP.mult)
                vtt(c_, sra[:], sra[:], OP.mult)
                vstt(srb[:], srb[:], 1.0 / D, c_, OP.mult, OP.subtract)
                mark(VE, sv)
                sv.wait(SE)
                SE.activation(srb[:], srb[:], AF.Sqrt, bias=ceps[0:1, 0:1], scale=1.0)
                mark(SE, ssm)
                ssm.wait(VE)
                VE.reciprocal(srb[:], srb[:])
                VE.drain()
                vtt(sra[:], sra[:], srb[:], OP.mult)
                mark(VE, sv)
                sv.wait(SE)
                sd2.inc(SE.dma_start(dsum[:, c0:c0 + 512], sra[:]), 16)
                sd2.inc(SE.dma_start(dsq[:, c0:c0 + 512], srb[:]), 16)

        def xn_sweep(src, ncols, rstdv, mrv, dsth, dstl):
            consumed2 = {0: 0, 1: 0}
            stored2 = {0: 0, 1: 0}
            for nt in range(ncols // 512):
                sv.wait(SY)
                sd2.wait(SY)
                sd.inc(SY.dma_start(srb[:], rstdv[:, 512 * nt:512 * (nt + 1)]), 16)
                sd.inc(SY.dma_start(sra[:], mrv[:, 512 * nt:512 * (nt + 1)]), 16)
                sv.wait(TE)
                sd.wait(TE)
                stm.inc(TE.matmul(ps4[:, :], ones_row_f[:], srb[:], start=True, stop=True))
                stm.inc(TE.matmul(ps5[:, :], ones_row_f[:], sra[:], start=True, stop=True))
                stm.wait(VE)
                vcp(bcast_a[:], ps4[:, :])
                vcp(bcast_b[:], ps5[:, :])
                mark(VE, sv)
                for ic in range(DC):
                    bi = ic % 2
                    buf = xa if bi == 0 else xb
                    th = thA if bi == 0 else thB
                    tl = tlA if bi == 0 else tlB
                    sdx = sdA if bi == 0 else sdB
                    s2x = s2A if bi == 0 else s2B
                    sv.wait(SY, consumed2[bi])
                    sdx.inc(SY.dma_start(buf[:], src[128 * ic:128 * (ic + 1), 512 * nt:512 * (nt + 1)]), 16)
                    sdx.wait(VE)
                    s2x.wait(VE, stored2[bi])
                    vtt(buf[:], buf[:], bcast_a[:], OP.mult)
                    vtt(buf[:], buf[:], bcast_b[:], OP.subtract)
                    vcp(th[:], buf[:])
                    vtt(tl[:], buf[:], th[:], OP.subtract)
                    consumed2[bi] = mark(VE, sv)
                    sv.wait(SE)
                    s2x.inc(SE.dma_start(dsth[128 * ic:128 * (ic + 1), 512 * nt:512 * (nt + 1)], th[:]), 16)
                    s2x.inc(SE.dma_start(dstl[128 * ic:128 * (ic + 1), 512 * nt:512 * (nt + 1)], tl[:]), 16)
                    stored2[bi] = s2x.n
                sv.wait(TE)

        stats_sweep(xcov, S, ps0, ps1, dsum, dsq)
        ln_finish(S)
        xn_sweep(xcov, S, dsq, dsum, xnch, xncl)
        barrier()
        stats_sweep(xown, NOWN, ps0, ps1, dsum, dsq)
        ln_finish(NOWN)
        xn_sweep(xown, NOWN, dsq, dsum, xnoh, xnol)
        barrier()

        def load_xpair(srch, srcl, col0):
            stm.wait(SY)
            sv.wait(SY)
            ncols = srch.tensor.shape[1]
            for ic4 in range(0, DC, 4):
                for dst, srcd in ((xph, srch), (xpl, srcl)):
                    sap = bass.AP(srcd.tensor, 128 * ic4 * ncols + col0,
                                  [[ncols, 128], [128 * ncols, 4], [1, 512]])
                    sdp.inc(SY.dma_start(
                        dst[:, 512 * ic4:512 * (ic4 + 4)].rearrange("p (c n) -> p c n", c=4),
                        sap), 16)

        def load_wpair(srch, srcl, ocol0, ow=128):
            stm.wait(SY)
            ncols = srch.tensor.shape[1]
            for ic4 in range(0, DC, 4):
                for dst, srcd in ((wph, srch), (wpl, srcl)):
                    sap = bass.AP(srcd.tensor, 128 * ic4 * ncols + ocol0,
                                  [[ncols, 128], [128 * ncols, 4], [1, ow]])
                    sdw.inc(SY.dma_start(
                        dst[:, ow * ic4:ow * (ic4 + 4)].rearrange("p (c n) -> p c n", c=4),
                        sap), 16)

        def mm3(psx, ow=128, nw=512):
            for ic in range(DC):
                lw = wph[:, ow * ic:ow * ic + ow]
                ll = wpl[:, ow * ic:ow * ic + ow]
                rh = xph[:, 512 * ic:512 * ic + nw]
                rl = xpl[:, 512 * ic:512 * ic + nw]
                stm.inc(TE.matmul(psx, lw, rh, start=(ic == 0), stop=False))
                stm.inc(TE.matmul(psx, lw, rl, start=False, stop=False))
                stm.inc(TE.matmul(psx, ll, rh, start=False, stop=(ic == DC - 1)))

        def rope_apply(psx, ctab, stab, col0):
            stm.wait(VE)
            sd2.wait(VE)
            vcp(tu[:], psx)
            mark(VE, sv)
            sv.wait(SE)
            sd2.inc(SE.dma_start(rsw[0:64, :], tu[64:128, :]), 16)
            sd2.inc(SE.dma_start(rsw[64:128, :], tu[0:64, :]), 16)
            sd2.wait(VE)
            vtt(tu[:], tu[:], ctab[:, col0:col0 + 512], OP.mult)
            vtt(rsw[:], rsw[:], stab[:, col0:col0 + 512], OP.mult)
            vtt(tu[:], tu[:], rsw[:], OP.add)
            vcp(thA[:], tu[:])
            vtt(tlA[:], tu[:], thA[:], OP.subtract)
            mark(VE, sv)

        for nt in range(S // 512):
            load_xpair(xnch, xncl, 512 * nt)
            sv.wait(GE)
            GE.iota(posk[:], [[1, 512]], base=512 * nt, channel_multiplier=0,
                    allow_small_or_imprecise_dtypes=True)
            mark(GE, sgm)
            sgm.wait(VE)
            for half, dst in ((0, ck), (1, sk)):
                vts(dst[:], posk[:], invf[:, 0:1], None, OP.mult)
                if half == 0:
                    vts(dst[:], dst[:], PI / 2, None, OP.add)
                rrange(dst, 512)
            mark(VE, sv)
            sv.wait(SE)
            sincol(ck[:]); sincol(sk[:])
            mark(SE, ssm)
            ssm.wait(VE)
            vts(sk[:], sk[:], csign[:, 0:1], None, OP.mult)
            mark(VE, sv)
            for kv in range(2):
                srch = wkh if kv == 0 else wvh
                srcl = wkl if kv == 0 else wvl
                for h in range(NKVH):
                    load_wpair(srch, srcl, 128 * h)
                    psx = ps0 if (h % 2 == 0) else ps1
                    sdw.wait(TE)
                    sdp.wait(TE)
                    sv.wait(TE)
                    mm3(psx[:, :])
                    if kv == 0:
                        rope_apply(psx[:, :], ck, sk, 0)
                        sv.wait(SE)
                        sd2.inc(SE.dma_start(kdh[h, :, 512 * nt:512 * (nt + 1)], thA[:]), 16)
                        sd2.inc(SE.dma_start(kdl[h, :, 512 * nt:512 * (nt + 1)], tlA[:]), 16)
                    else:
                        stm.wait(VE)
                        sd2.wait(VE)
                        vcp(tu[:], psx[:, :])
                        vcp(thA[:], tu[:])
                        vtt(tlA[:], tu[:], thA[:], OP.subtract)
                        mark(VE, sv)
                        sv.wait(TE)
                        for j in range(4):
                            for src_t, dst_d in ((thA, vdh), (tlA, vdl)):
                                sd2.wait(TE)
                                stm.inc(TE.transpose(pbt[:, :], src_t[:, 128 * j:128 * (j + 1)], identb[:]))
                                stm.wait(VE)
                                vcp(thB[:, 0:128], pbt[:, :])
                                mark(VE, sv)
                                sv.wait(SE)
                                sd2.inc(SE.dma_start(
                                    dst_d[h, 512 * nt + 128 * j:512 * nt + 128 * (j + 1), :],
                                    thB[:, 0:128]), 16)
                        sd2.wait(VE)
        barrier()

        for nt in range(NOWN // 512):
            load_xpair(xnoh, xnol, 512 * nt)
            sdp.wait(TE)
            for h in range(NQH):
                load_wpair(wqh, wql, 128 * h)
                psx = ps0 if (h % 2 == 0) else ps1
                sdw.wait(TE)
                sv.wait(TE)
                mm3(psx[:, :])
                rope_apply(psx[:, :], cq, sq, 512 * nt)
                sv.wait(SE)
                g, hj = h // 4, h % 4
                sd2.inc(SE.dma_start(qdh[g, :, hj, 512 * nt:512 * (nt + 1)], thA[:]), 16)
                sd2.inc(SE.dma_start(qdl[g, :, hj, 512 * nt:512 * (nt + 1)], tlA[:]), 16)
        barrier()

        for ot in range(DC):
            sv.wait(SY)
            sdl.inc(SY.dma_start(lutf[:], lut4_i[128 * ot:128 * (ot + 1), :]), 16)
            sdl.wait(VE)
            sd2.wait(VE)
            vcp(eqt[:, 0:16], lutf[:])
            vcp(luth[:], eqt[:, 0:16])
            vtt(lutl[:], lutf[:], luth[:], OP.subtract)
            for icb in range(4):
                sv.wait(SY)
                sdx = sdA if icb % 2 == 0 else sdB
                for p in range(4):
                    sdx.inc(SY.dma_start(h4t[:, 1024 * p:1024 * (p + 1)],
                                        h4_i[p, 128 * ot:128 * (ot + 1), 1024 * icb:1024 * (icb + 1)]), 16)
                sdx.wait(VE)
                for p in range(4):
                    VE.tensor_tensor(sh16[:], h4t[:, 1024 * p:1024 * (p + 1)], jt16[:],
                                     OP.logical_shift_right)
                    VE.drain()
                    if p == 0:
                        vts(ix16[:], sh16[:], 1, None, OP.bitwise_and)
                    else:
                        vts(sh16[:], sh16[:], 1, p, OP.bitwise_and, OP.logical_shift_left)
                        vtt(ix16[:], ix16[:], sh16[:], OP.bitwise_or)
                dho = dwh[:, 1024 * icb:1024 * (icb + 1)]
                dlo = dwl[:, 1024 * icb:1024 * (icb + 1)]
                for c in range(16):
                    vts(eqt[:], ix16[:], float(c), None, OP.is_equal)
                    if c == 0:
                        vts(dho, eqt[:], luth[:, 0:1], None, OP.mult)
                        vts(dlo, eqt[:], lutl[:, 0:1], None, OP.mult)
                    else:
                        vstt(dho, eqt[:], luth[:, c:c + 1], dho, OP.mult, OP.add)
                        vstt(dlo, eqt[:], lutl[:, c:c + 1], dlo, OP.mult, OP.add)
                mark(VE, sv)
            sv.wait(SE)
            sd2.inc(SE.dma_start(wlh[128 * ot:128 * (ot + 1), :], dwh[:]), 16)
            sd2.inc(SE.dma_start(wll[128 * ot:128 * (ot + 1), :], dwl[:]), 16)
        barrier()

        for nt in range(NOWN // 512):
            load_xpair(xnoh, xnol, 512 * nt)
            sdp.wait(TE)
            for oc in range(DC):
                stm.wait(SY)
                sdw.inc(SY.dma_start_transpose(
                    wph[:].rearrange("p (m l) -> p m l", l=128),
                    wlh[128 * oc:128 * (oc + 1), :]), 16)
                sdw.inc(SY.dma_start_transpose(
                    wpl[:].rearrange("p (m l) -> p m l", l=128),
                    wll[128 * oc:128 * (oc + 1), :]), 16)
                psx = ps0 if (oc % 2 == 0) else ps1
                sdw.wait(TE)
                sv.wait(TE)
                mm3(psx[:, :])
                stm.wait(VE)
                sd2.wait(VE)
                vts(tu[:], psx[:, :], bcol[:, oc:oc + 1], None, OP.add)
                mark(VE, sv)
                sv.wait(SE)
                sd2.inc(SE.dma_start(yffn[128 * oc:128 * (oc + 1), 512 * nt:512 * (nt + 1)], tu[:]), 16)
        barrier()

        kbufs = ((kthA, ktlA, vthA, vtlA), (kthB, ktlB, vthB, vtlB))
        prbufs = ((prhA, prlA), (prhB, prlB))
        for k in range(NQT):
            E = 4 * (k + 1)
            sv.wait(SY)
            for l in range(4):
                sdm.inc(SY.dma_start(mq4[:, 512 * l:512 * (l + 1)], mqm_i[k, l, :, :]), 16)
            for g in range(NKVH):
                stm.wait(SY)
                sdq.inc(SY.dma_start(qth[:], qdh[g, :, :, 128 * k:128 * (k + 1)]), 16)
                sdq.inc(SY.dma_start(qtl[:], qdl[g, :, :, 128 * k:128 * (k + 1)]), 16)
                te_used = {0: 0, 1: 0}
                ve_done = {0: 0, 1: 0}
                se_done = {0: 0, 1: 0}
                for j in range(E):
                    bi = j % 2
                    kth, ktl, vth_t, vtl_t = kbufs[bi]
                    prh, prl = prbufs[bi]
                    sdx = sdA if bi == 0 else sdB
                    stm.wait(SY, te_used[bi])
                    sdx.inc(SY.dma_start(kth[:], kdh[g, :, 128 * j:128 * (j + 1)]), 16)
                    sdx.inc(SY.dma_start(ktl[:], kdl[g, :, 128 * j:128 * (j + 1)]), 16)
                    sdx.inc(SY.dma_start(vth_t[:], vdh[g, 128 * j:128 * (j + 1), :]), 16)
                    sdx.inc(SY.dma_start(vtl_t[:], vdl[g, 128 * j:128 * (j + 1), :]), 16)
                    psc = ps0 if bi == 0 else ps1
                    sdx.wait(TE)
                    sdq.wait(TE)
                    ssm.wait(TE, se_done[bi])
                    stm.inc(TE.matmul(psc[:, :], kth[:], qth[:], start=True, stop=False))
                    stm.inc(TE.matmul(psc[:, :], kth[:], qtl[:], start=False, stop=False))
                    stm.inc(TE.matmul(psc[:, :], ktl[:], qth[:], start=False, stop=True))
                    scores_done = stm.n
                    if j >= 4 * k:
                        l = j - 4 * k
                        stm.wait(VE, scores_done)
                        sdm.wait(VE)
                        ssm.wait(VE, se_done[bi])
                        vtt(psc[:, :], psc[:, :], mq4[:, 512 * l:512 * (l + 1)], OP.add)
                        mvd = mark(VE, sv)
                        sv.wait(SE, mvd)
                    else:
                        stm.wait(SE, scores_done)
                    sv.wait(SE, max(ve_done[0], ve_done[1]))
                    SE.activation(prf[:], psc[:, :], AF.Exp)
                    se_done[bi] = mark(SE, ssm)
                    ssm.wait(VE)
                    vcp(prh[:], prf[:])
                    vtt(prl[:], prf[:], prh[:], OP.subtract)
                    ve_done[bi] = mark(VE, sv)
                    sv.wait(TE, ve_done[bi])
                    stm.inc(TE.matmul(ps3[0:1, :], ones_col_b[:], prh[:], start=(j == 0), stop=False))
                    stm.inc(TE.matmul(ps3[0:1, :], ones_col_b[:], prl[:], start=False, stop=(j == E - 1)))
                    stm.inc(TE.matmul(ps2[:, :], vth_t[:], prh[:], start=(j == 0), stop=False))
                    stm.inc(TE.matmul(ps2[:, :], vth_t[:], prl[:], start=False, stop=False))
                    stm.inc(TE.matmul(ps2[:, :], vtl_t[:], prh[:], start=False, stop=(j == E - 1)))
                    te_used[bi] = stm.n
                stm.wait(VE)
                vcp(dnv[:], ps3[0:1, :])
                VE.reciprocal(dnv[:], dnv[:])
                mark(VE, sv)
                sv.wait(TE)
                stm.inc(TE.matmul(ps4[:, :], ones_row_f[:], dnv[:], start=True, stop=True))
                stm.wait(VE)
                vcp(dnb[:], ps4[:, :])
                vtt(tu[:], ps2[:, :], dnb[:], OP.mult)
                for hj in range(4):
                    f = 4 * g + hj
                    vcp(aoph[:, 128 * f:128 * (f + 1)], tu[:, 128 * hj:128 * (hj + 1)])
                    vtt(aopl[:, 128 * f:128 * (f + 1)], tu[:, 128 * hj:128 * (hj + 1)],
                        aoph[:, 128 * f:128 * (f + 1)], OP.subtract)
                mark(VE, sv)
                sv.wait(TE)
            sv.wait(SE)
            sd2.wait(SE)
            for part, dst in ((aoph, aodh), (aopl, aodl)):
                dst_ap = bass.AP(dst.tensor, 128 * k, [[NOWN, 128], [128 * NOWN, DC], [1, 128]])
                sd2.inc(SE.dma_start(dst_ap, part[:].rearrange("d (f q) -> d f q", q=128)), 16)
            sd2.wait(VE)
        barrier()

        for nt in range(NOWN // 512):
            load_xpair(aodh, aodl, 512 * nt)
            sdp.wait(TE)
            for oc in range(DC):
                load_wpair(woh, wol, 128 * oc)
                psx = ps0 if (oc % 2 == 0) else ps1
                sdw.wait(TE)
                sv.wait(TE)
                ssm.wait(TE)
                mm3(psx[:, :])
                mmdone = stm.n
                sv.wait(SY)
                sdq.inc(SY.dma_start(xa[:], xown[128 * oc:128 * (oc + 1), 512 * nt:512 * (nt + 1)]), 16)
                sdm.inc(SY.dma_start(xb[:], yffn[128 * oc:128 * (oc + 1), 512 * nt:512 * (nt + 1)]), 16)
                stm.wait(VE, mmdone)
                sdq.wait(VE)
                sdm.wait(VE)
                sd2.wait(VE)
                ssm.wait(VE)
                vtt(tu[:], psx[:, :], xa[:], OP.add)
                vtt(tu[:], tu[:], xb[:], OP.add)
                mark(VE, sv)
                sv.wait(SE)
                sd2.inc(SE.dma_start(x1d[128 * oc:128 * (oc + 1), 512 * nt:512 * (nt + 1)], tu[:]), 16)
                SE.activation(tv[:], tu[:], AF.Square)
                mark(SE, ssm)
                sv.wait(TE)
                stm.inc(TE.matmul(ps4[0:1, :], ones_col_f[:], tu[:], start=(oc == 0), stop=(oc == DC - 1)))
                ssm.wait(TE)
                stm.inc(TE.matmul(ps5[0:1, :], ones_col_f[:], tv[:], start=(oc == 0), stop=(oc == DC - 1)))
            stm.wait(VE)
            sd2.wait(VE)
            vcp(sra[:], ps4[0:1, :])
            vcp(srb[:], ps5[0:1, :])
            mark(VE, sv)
            sv.wait(TE)
            sv.wait(SE)
            sd2.inc(SE.dma_start(dsum[:, 512 * nt:512 * (nt + 1)], sra[:]), 16)
            sd2.inc(SE.dma_start(dsq[:, 512 * nt:512 * (nt + 1)], srb[:]), 16)
        ln_finish(NOWN)
        xn_sweep(x1d, NOWN, dsq, dsum, x1nh, x1nl)
        barrier()

        sdl.inc(SY.dma_start(tu[0:1, 0:1], ls_i), 16)
        sdl.inc(SY.dma_start(tu[0:1, 1:2], dd_i), 16)
        sdl.wait(SE)
        SE.activation(tu[0:1, 0:1], tu[0:1, 0:1], AF.Exp)
        mark(SE, ssm)
        ssm.wait(VE)
        VE.reciprocal(tu[0:1, 0:1], tu[0:1, 0:1])
        mark(VE, sv)
        sv.wait(TE)
        stm.inc(TE.matmul(ps4[:, 0:2], ones_row_f[:], tu[0:1, 0:2], start=True, stop=True))
        stm.wait(VE)
        vcp(stepc[:], ps4[:, 0:1])
        vcp(dcol[:], ps4[:, 1:2])
        mark(VE, sv)
        sv.wait(GE)
        GE.iota(omre[:], [[1, 32]], base=0, channel_multiplier=32,
                allow_small_or_imprecise_dtypes=True)
        mark(GE, sgm)
        sgm.wait(VE)
        vts(t32d[:], omre[:], float(-TWO_PI / 4096.0), None, OP.mult)
        vts(omre[:], t32d[:], PI / 2, None, OP.add)
        vcp(omim[:], t32d[:])
        rrange(omre, 32)
        rrange(omim, 32)
        mark(VE, sv)
        sv.wait(SE)
        sincol(omre[:]); sincol(omim[:])
        mark(SE, ssm)
        ssm.wait(VE)
        vts(t32a[:], omre[:], 1.0, None, OP.add)
        vtt(t32b[:], t32a[:], t32a[:], OP.mult)
        vtt(t32c[:], omim[:], omim[:], OP.mult)
        vtt(t32b[:], t32b[:], t32c[:], OP.add)
        VE.reciprocal(t32b[:], t32b[:])
        VE.drain()
        vtt(gre[:], omre[:], omre[:], OP.mult)
        vtt(gre[:], gre[:], t32c[:], OP.add)
        vts(gre[:], gre[:], -1.0, 1.0, OP.mult, OP.add)
        vtt(gre[:], gre[:], t32b[:], OP.mult)
        vts(gre[:], gre[:], stepc[:, 0:1], 2.0, OP.mult, OP.mult)
        vtt(gim[:], omim[:], t32b[:], OP.mult)
        vts(gim[:], gim[:], stepc[:, 0:1], -4.0, OP.mult, OP.mult)
        vtt(cre[:], t32a[:], t32b[:], OP.mult)
        vts(cre[:], cre[:], 2.0, None, OP.mult)
        vtt(cim[:], omim[:], t32b[:], OP.mult)
        vts(cim[:], cim[:], -2.0, None, OP.mult)
        mark(VE, sv)
        sd.inc(SY.dma_start(s4p[:], bp_i), 16)
        sd.inc(SY.dma_start(s4v[:], vc_reT), 16)
        sd.inc(SY.dma_start(s4w[:], vc_imT), 16)
        for i2, src_i in enumerate((ct_re, ct_im, p_re, p_im, q_re, q_im, lam_re, lam_im)):
            sd.inc(SY.dma_start(sm64[i2][:], src_i), 16)
        sd.wait(TE)
        sv.wait(TE)
        stm.inc(TE.matmul(ps4[0:1, 0:64], s4p[:], s4v[:], start=True, stop=True))
        stm.inc(TE.matmul(ps5[0:1, 0:64], s4p[:], s4w[:], start=True, stop=True))
        stm.wait(VE)
        vcp(brea[:], ps4[0:1, 0:64])
        vcp(bima[:], ps5[0:1, 0:64])

        def cmulconj(ar_, ai_, br_, bi_, or_, oi_):
            vtt(or_[:], ar_[:], br_[:], OP.mult)
            vtt(tr9[:], ai_[:], bi_[:], OP.mult)
            vtt(or_[:], or_[:], tr9[:], OP.add)
            vtt(oi_[:], ar_[:], bi_[:], OP.mult)
            vtt(tr9[:], ai_[:], br_[:], OP.mult)
            vtt(oi_[:], oi_[:], tr9[:], OP.subtract)

        sd.wait(VE)
        cmulconj(sm64[0], sm64[1], brea, bima, r1, r2)
        cmulconj(sm64[0], sm64[1], sm64[2], sm64[3], r3, r4)
        cmulconj(sm64[4], sm64[5], brea, bima, r5, r6)
        cmulconj(sm64[4], sm64[5], sm64[2], sm64[3], r7, r8)
        mark(VE, sv)
        sv.wait(TE)
        for i2, (src_row, dst) in enumerate(((r1, n1r), (r2, n1i), (r3, n2r), (r4, n2i),
                                             (r5, n3r), (r6, n3i), (r7, n4r), (r8, n4i),
                                             (sm64[6], lrb), (sm64[7], lib))):
            pp = ps4 if (i2 % 2 == 0) else ps5
            stm.inc(TE.matmul(pp[:, 0:64], ones_row_f[:], src_row[:], start=True, stop=True))
            stm.wait(VE)
            vcp(dst[:], pp[:, 0:64])
            mark(VE, sv)
            sv.wait(TE)
        for b in range(32):
            vts(t64a[:], lrb[:], gre[:, b:b + 1], None, OP.subtract)
            vts(t64b[:], lib[:], gim[:, b:b + 1], None, OP.subtract)
            vtt(t64c[:], t64a[:], t64a[:], OP.mult)
            vtt(t64d[:], t64b[:], t64b[:], OP.mult)
            vtt(t64c[:], t64c[:], t64d[:], OP.add)
            VE.reciprocal(t64c[:], t64c[:])
            VE.drain()
            for (nr, ni, outr, outi) in ((n1r, n1i, k00r, k00i), (n2r, n2i, k01r, k01i),
                                         (n3r, n3i, k10r, k10i), (n4r, n4i, k11r, k11i)):
                vtt(t64d[:], nr[:], t64a[:], OP.mult)
                vtt(t64e[:], ni[:], t64b[:], OP.mult)
                vtt(t64d[:], t64d[:], t64e[:], OP.add)
                vtt(t64d[:], t64d[:], t64c[:], OP.mult)
                VE.tensor_reduce(outr[:, b:b + 1], t64d[:], AX, OP.add, negate=True)
                VE.drain()
                vtt(t64d[:], ni[:], t64a[:], OP.mult)
                vtt(t64e[:], nr[:], t64b[:], OP.mult)
                vtt(t64d[:], t64d[:], t64e[:], OP.subtract)
                vtt(t64d[:], t64d[:], t64c[:], OP.mult)
                VE.tensor_reduce(outi[:, b:b + 1], t64d[:], AX, OP.add, negate=True)
                VE.drain()
        vts(t32a[:], k11r[:], 1.0, None, OP.add)
        vtt(t32b[:], t32a[:], t32a[:], OP.mult)
        vtt(t32c[:], k11i[:], k11i[:], OP.mult)
        vtt(t32b[:], t32b[:], t32c[:], OP.add)
        VE.reciprocal(t32b[:], t32b[:])
        VE.drain()
        vtt(arr[:], k01r[:], k10r[:], OP.mult)
        vtt(t32c[:], k01i[:], k10i[:], OP.mult)
        vtt(arr[:], arr[:], t32c[:], OP.subtract)
        vtt(ari[:], k01r[:], k10i[:], OP.mult)
        vtt(t32c[:], k01i[:], k10r[:], OP.mult)
        vtt(ari[:], ari[:], t32c[:], OP.add)
        vtt(t32c[:], arr[:], t32a[:], OP.mult)
        vtt(t32d[:], ari[:], k11i[:], OP.mult)
        vtt(t32c[:], t32c[:], t32d[:], OP.add)
        vtt(t32c[:], t32c[:], t32b[:], OP.mult)
        vtt(t32d[:], ari[:], t32a[:], OP.mult)
        vtt(ari[:], arr[:], k11i[:], OP.mult)
        vtt(t32d[:], t32d[:], ari[:], OP.subtract)
        vtt(t32d[:], t32d[:], t32b[:], OP.mult)
        vtt(arr[:], k00r[:], t32c[:], OP.subtract)
        vtt(ari[:], k00i[:], t32d[:], OP.subtract)
        vtt(t32a[:], arr[:], cre[:], OP.mult)
        vtt(t32b[:], ari[:], cim[:], OP.mult)
        vtt(t32c[:], arr[:], cim[:], OP.mult)
        vtt(t32d[:], ari[:], cre[:], OP.mult)
        vtt(arr[:], t32a[:], t32b[:], OP.subtract)
        vtt(ari[:], t32c[:], t32d[:], OP.add)
        vts(ariN[:], ari[:], -1.0, None, OP.mult)
        mark(VE, sv)
        if dbg_s4:
            sv.wait(SE)
            for i2, t_ in enumerate((omre, omim, gre, gim, cre, cim, k00r, k00i, k11r, k11i, arr, ari)):
                sd2.inc(SE.dma_start(s4dbg[i2, :, :], t_[:]), 16)
            for i2, t_ in enumerate((lrb, lib, n1r, n1i)):
                sd2.inc(SE.dma_start(s4dbg2[i2, :, :], t_[:]), 16)
            sd2.wait(VE)
        sv.wait(GE)
        GE.iota(tu[:, 0:128], [[1, 128]], base=0, channel_multiplier=0,
                allow_small_or_imprecise_dtypes=True)
        GE.iota(tv[:, 0:32], [[1, 32]], base=0, channel_multiplier=0,
                allow_small_or_imprecise_dtypes=True)
        mark(GE, sgm)
        sgm.wait(VE)
        vts(tw[:, 0:128], tu[:, 0:128], pcol[:, 0:1], float(TWO_PI / 128.0), OP.mult, OP.mult)
        vts(wc128[:], tw[:, 0:128], PI / 2, None, OP.add)
        vcp(ws128[:], tw[:, 0:128])
        rrange(wc128, 128)
        rrange(ws128, 128)
        vts(tw[:, 0:32], tv[:, 0:32], pcol[:, 0:1], float(TWO_PI / 4096.0), OP.mult, OP.mult)
        vts(twc[:], tw[:, 0:32], PI / 2, None, OP.add)
        vcp(tws[:], tw[:, 0:32])
        rrange(twc, 32)
        rrange(tws, 32)
        vts(tw[0:32, 0:32], tv[0:32, 0:32], pcol[0:32, 0:1], float(TWO_PI / 32.0), OP.mult, OP.mult)
        vts(c32s[:], tw[0:32, 0:32], PI / 2, None, OP.add)
        vcp(s32sn[:], tw[0:32, 0:32])
        rrange(c32s, 32, npart=32)
        rrange(s32sn, 32, npart=32)
        mark(VE, sv)
        sv.wait(SE)
        sincol(wc128[:]); sincol(ws128[:]); sincol(twc[:]); sincol(tws[:])
        sincol(c32s[:], 32); sincol(s32sn[:], 32)
        mark(SE, ssm)
        ssm.wait(VE)
        vts(c32s[:], c32s[:], float(1.0 / 4096.0), None, OP.mult)
        vts(s32sn[:], s32sn[:], float(-1.0 / 4096.0), None, OP.mult)
        mark(VE, sv)
        sv.wait(TE)
        stm.inc(TE.matmul(ps4[:, 0:32], wc128[:], arr[:], start=True, stop=False))
        stm.inc(TE.matmul(ps4[:, 0:32], ws128[:], ariN[:], start=False, stop=True))
        stm.inc(TE.matmul(ps5[:, 0:32], ws128[:], arr[:], start=True, stop=False))
        stm.inc(TE.matmul(ps5[:, 0:32], wc128[:], ari[:], start=False, stop=True))
        stm.wait(VE)
        vcp(yre[:], ps4[:, 0:32])
        vcp(yim[:], ps5[:, 0:32])
        vtt(t32a[:], yre[:], twc[:], OP.mult)
        vtt(t32b[:], yim[:], tws[:], OP.mult)
        vtt(t32c[:], yre[:], tws[:], OP.mult)
        vtt(t32d[:], yim[:], twc[:], OP.mult)
        vtt(yre[:], t32a[:], t32b[:], OP.subtract)
        vtt(yim[:], t32c[:], t32d[:], OP.add)
        mark(VE, sv)
        sv.wait(TE)
        stm.inc(TE.transpose(pft[0:32, 0:128], yre[:], identf[:]))
        stm.wait(VE)
        vcp(yreT[:], pft[0:32, 0:128])
        mark(VE, sv)
        sv.wait(TE)
        stm.inc(TE.transpose(pft[0:32, 0:128], yim[:], identf[:]))
        stm.wait(VE)
        vcp(yimT[:], pft[0:32, 0:128])
        mark(VE, sv)
        sv.wait(TE)
        stm.inc(TE.matmul(ps4[0:32, 0:128], c32s[:], yreT[:], start=True, stop=False))
        stm.inc(TE.matmul(ps4[0:32, 0:128], s32sn[:], yimT[:], start=False, stop=True))
        stm.wait(VE)
        vcp(ktap[:], ps4[0:32, 0:128])
        mark(VE, sv)
        if dbg_s4:
            sv.wait(SE)
            sd2.inc(SE.dma_start(s4dbg3[0, :, :], wc128[:]), 16)
            sd2.inc(SE.dma_start(s4dbg3[1, :, :], ws128[:]), 16)
            sd2.inc(SE.dma_start(s4dbg3[2, :, 0:32], twc[:]), 16)
            sd2.inc(SE.dma_start(s4dbg3[3, :, 0:32], tws[:]), 16)
            sd2.inc(SE.dma_start(s4dbg3[4, 0:32, 0:32], c32s[:]), 16)
            sd2.inc(SE.dma_start(s4dbg3[5, 0:32, 0:32], s32sn[:]), 16)
            sd2.inc(SE.dma_start(s4dbg3[6, :, 0:32], yre[:]), 16)
            sd2.inc(SE.dma_start(s4dbg3[7, :, 0:32], yim[:]), 16)
            sd2.wait(VE)
        sv.wait(SE)
        sd2.inc(SE.dma_start(bass.AP(kpad.tensor, 0, [[1, 128]]), zrow[:]), 16)
        sd2.inc(SE.dma_start(bass.AP(kpad.tensor, 128, [[128, 32], [1, 128]]), ktap[:]), 16)
        sd2.wait(SY)
        for d_ in range(32):
            srcap = bass.AP(kpad.tensor, 1 + 128 * d_, [[1, 128], [1, 128]])
            stm.wait(SY)
            sd.inc(SY.dma_start(bdf[:], srcap), 16)
            sd.wait(TE)
            sv.wait(TE)
            stm.inc(TE.matmul(pft[:, 0:128], revf[:], bdf[:], start=True, stop=True))
            stm.wait(VE)
            vcp(bdh[:, 128 * d_:128 * (d_ + 1)], pft[:, 0:128])
            vtt(bdl[:, 128 * d_:128 * (d_ + 1)], pft[:, 0:128],
                bdh[:, 128 * d_:128 * (d_ + 1)], OP.subtract)
            mark(VE, sv)
        barrier()

        for nt in range(NOWN // 512):
            load_xpair(x1nh, x1nl, 512 * nt)
            stm.wait(SY)
            load_wpair(wrdh, wrdl, 0, ow=64)
            sdw.wait(TE)
            sdp.wait(TE)
            sv.wait(TE)
            for ic in range(DC):
                lw = wph[:, 64 * ic:64 * (ic + 1)]
                ll = wpl[:, 64 * ic:64 * (ic + 1)]
                rh = xph[:, 512 * ic:512 * (ic + 1)]
                rl = xpl[:, 512 * ic:512 * (ic + 1)]
                stm.inc(TE.matmul(ps0[0:64, :], lw, rh, start=(ic == 0), stop=False))
                stm.inc(TE.matmul(ps0[0:64, :], lw, rl, start=False, stop=False))
                stm.inc(TE.matmul(ps0[0:64, :], ll, rh, start=False, stop=(ic == DC - 1)))
            stm.wait(VE)
            vcp(tu[0:64, :], ps0[0:64, :])
            mark(VE, sv)
            sv.wait(TE)
            for jt in range(4):
                stm.inc(TE.transpose(pft[0:128, 0:64], tu[0:64, 128 * jt:128 * (jt + 1)], identf[0:64, 0:64]))
                stm.wait(VE)
                vcp(sc64[:], pft[:, 0:64])
                VE.max(m8a[:], sc64[:])
                VE.drain()
                vcp(sc64b[:], sc64[:])
                VE.match_replace(sc64b[:], m8a[:], sc64b[:], NEG)
                VE.drain()
                VE.max(m8b[:], sc64b[:])
                VE.drain()
                vts(msk64[:], sc64[:], m8b[:, 7:8], None, OP.is_ge)
                mark(VE, sv)
                sv.wait(SE)
                SE.activation(g64[:], sc64[:], AF.Exp)
                mark(SE, ssm)
                ssm.wait(VE)
                vtt(g64[:], g64[:], msk64[:], OP.mult)
                VE.tensor_reduce(dn64[:], g64[:], AX, OP.add)
                VE.drain()
                VE.reciprocal(dn64[:], dn64[:])
                VE.drain()
                vts(g64[:], g64[:], dn64[:, 0:1], 2.0, OP.mult, OP.mult)
                mark(VE, sv)
                sv.wait(TE)
                stm.inc(TE.transpose(pft[0:64, 0:128], g64[:], identf[:]))
                stm.wait(VE)
                vcp(gt64[:, 128 * jt:128 * (jt + 1)], pft[0:64, 0:128])
                mark(VE, sv)
                sv.wait(TE)
            for t_ in range(8):
                stm.inc(TE.matmul(ps1[:, :], e8f[:, 128 * t_:128 * (t_ + 1)], gt64[:], start=True, stop=True))
                stm.wait(VE)
                vcp(gxf[:, 512 * t_:512 * (t_ + 1)], ps1[:, :])
                mark(VE, sv)
                sv.wait(TE)
            for t_ in range(8):
                psx = ps2 if (t_ % 2 == 0) else ps3
                sv.wait(TE)
                stm.wait(SY)
                for ic in range(DC):
                    sdw.inc(SY.dma_start(wph[:, 128 * ic:128 * (ic + 1)],
                                         arh[128 * ic:128 * (ic + 1), 128 * t_:128 * (t_ + 1)]), 16)
                    sdw.inc(SY.dma_start(wpl[:, 128 * ic:128 * (ic + 1)],
                                         arl[128 * ic:128 * (ic + 1), 128 * t_:128 * (t_ + 1)]), 16)
                sdw.wait(TE)
                mm3(psx[:, :])
                stm.wait(VE)
                sout.wait(VE)
                vtt(tv[:], psx[:, :], gxf[:, 512 * t_:512 * (t_ + 1)], OP.mult)
                vcp(hgh[:, 512 * t_:512 * (t_ + 1)], tv[:])
                vtt(hgl[:, 512 * t_:512 * (t_ + 1)], tv[:], hgh[:, 512 * t_:512 * (t_ + 1)], OP.subtract)
                mark(VE, sv)
            for jc in range(DC):
                psx = ps4 if (jc % 2 == 0) else ps5
                sv.wait(TE)
                first = True
                for i_ in range(jc + 1):
                    d_ = jc - i_
                    lwh = bdh[:, 128 * d_:128 * (d_ + 1)]
                    lwl = bdl[:, 128 * d_:128 * (d_ + 1)]
                    rh = xph[:, 512 * i_:512 * (i_ + 1)]
                    rl = xpl[:, 512 * i_:512 * (i_ + 1)]
                    stm.inc(TE.matmul(psx[:, :], lwh, rh, start=first, stop=False))
                    stm.inc(TE.matmul(psx[:, :], lwh, rl, start=False, stop=False))
                    stm.inc(TE.matmul(psx[:, :], lwl, rh, start=False, stop=False))
                    first = False
                for t_ in range(8):
                    w0 = 128 * (t_ % 2)
                    stm.wait(SY)
                    sdw.inc(SY.dma_start(wph[:, w0:w0 + 128],
                                         brh[128 * t_:128 * (t_ + 1), 128 * jc:128 * (jc + 1)]), 16)
                    sdw.inc(SY.dma_start(wpl[:, w0:w0 + 128],
                                         brl[128 * t_:128 * (t_ + 1), 128 * jc:128 * (jc + 1)]), 16)
                    sdw.wait(TE)
                    hh = hgh[:, 512 * t_:512 * (t_ + 1)]
                    hl = hgl[:, 512 * t_:512 * (t_ + 1)]
                    last = (t_ == 7)
                    stm.inc(TE.matmul(psx[:, :], wph[:, w0:w0 + 128], hh, start=False, stop=False))
                    stm.inc(TE.matmul(psx[:, :], wph[:, w0:w0 + 128], hl, start=False, stop=False))
                    stm.inc(TE.matmul(psx[:, :], wpl[:, w0:w0 + 128], hh, start=False, stop=last))
                mmdone = stm.n
                sv.wait(SY)
                sdm.inc(SY.dma_start(xa[:], x1d[128 * jc:128 * (jc + 1), 512 * nt:512 * (nt + 1)]), 16)
                stm.wait(VE, mmdone)
                sdm.wait(VE)
                sout.wait(VE)
                VE.scalar_tensor_tensor(tv[:], xph[:, 512 * jc:512 * (jc + 1)], dcol[:, 0:1],
                                        psx[:, :], OP.mult, OP.add)
                VE.drain()
                VE.scalar_tensor_tensor(tv[:], xpl[:, 512 * jc:512 * (jc + 1)], dcol[:, 0:1],
                                        tv[:], OP.mult, OP.add)
                VE.drain()
                vtt(tv[:], tv[:], xa[:], OP.add)
                mark(VE, sv)
                sv.wait(SE)
                sout.inc(SE.dma_start(yout[128 * jc:128 * (jc + 1), 512 * nt:512 * (nt + 1)], tv[:]), 16)
        sout.wait(SE)

    for c in reversed(ctxs):
        c.__exit__(None, None, None)
    return nc



def _stage(inputs):
    import ml_dtypes
    x = np.asarray(inputs["L0_attn_x"])
    wq = np.ascontiguousarray(np.asarray(inputs["L0_attn_Wq"], np.float32))
    wk = np.ascontiguousarray(np.asarray(inputs["L0_attn_Wk"], np.float32))
    wv = np.ascontiguousarray(np.asarray(inputs["L0_attn_Wv"], np.float32))
    wo = np.ascontiguousarray(np.asarray(inputs["L0_attn_Wo"], np.float32))
    qw = np.ascontiguousarray(np.asarray(inputs["L0_ffn_qweight"], np.int32))
    h4 = np.ascontiguousarray(qw[0:4]).view("<u2").reshape(4, D, 256)
    h4 = np.ascontiguousarray(np.repeat(h4, 16, axis=2))
    lut4 = np.ascontiguousarray(np.asarray(inputs["L0_ffn_lut4"], np.float32))
    bias0 = np.ascontiguousarray(np.asarray(inputs["L0_ffn_bias"], np.float32).reshape(DC, 128).T)
    lam = np.asarray(inputs["L1_attn_Lambda_ri"], np.float32)
    pri = np.asarray(inputs["L1_attn_p_ri"], np.float32)
    qri = np.asarray(inputs["L1_attn_q_ri"], np.float32)
    vc = np.asarray(inputs["L1_attn_Vc_ri"], np.float32)
    ct = np.asarray(inputs["L1_attn_Ct_ri"], np.float32)
    bp = np.asarray(inputs["L1_attn_Bp"], np.float32)
    dd = np.asarray(inputs["L1_attn_D"], np.float32)
    ls = np.asarray(inputs["L1_attn_log_step"], np.float32)
    A = np.asarray(inputs["L1_ffn_A"], np.float32)
    B = np.asarray(inputs["L1_ffn_B"], np.float32)
    Wr = np.asarray(inputs["L1_ffn_Wr"], np.float32)
    a_r = np.ascontiguousarray(A.transpose(2, 0, 1).reshape(D, 1024))
    b_r = np.ascontiguousarray(B.transpose(0, 2, 1).reshape(1024, D))
    wr_r = np.ascontiguousarray(Wr.T)
    e8 = np.zeros((64, 1024), np.float32)
    for k in range(64):
        e8[k, k * 16:(k + 1) * 16] = 1.0

    shared = dict(
        wq=wq, wk=wk, wv=wv, wo=wo, h4=h4, lut4=lut4, bias0=bias0,
        lam_re=np.ascontiguousarray(lam[:, 0]).reshape(1, 64),
        lam_im=np.ascontiguousarray(lam[:, 1]).reshape(1, 64),
        p_re=np.ascontiguousarray(pri[:, 0]).reshape(1, 64),
        p_im=np.ascontiguousarray(pri[:, 1]).reshape(1, 64),
        q_re=np.ascontiguousarray(qri[:, 0]).reshape(1, 64),
        q_im=np.ascontiguousarray(qri[:, 1]).reshape(1, 64),
        ct_re=np.ascontiguousarray(ct[0, :, 0]).reshape(1, 64),
        ct_im=np.ascontiguousarray(ct[0, :, 1]).reshape(1, 64),
        vc_reT=np.ascontiguousarray(vc[:, :, 0].T),
        vc_imT=np.ascontiguousarray(vc[:, :, 1].T),
        bp=np.ascontiguousarray(bp).reshape(64, 1),
        dsc_d=np.ascontiguousarray(dd).reshape(1, 1),
        lstep=np.ascontiguousarray(ls).reshape(1, 1),
        a_r=a_r, b_r=b_r, wr_r=wr_r, e8=e8)

    maps = []
    for core in range(8):
        b = core // CPB
        c = core % CPB
        xT = np.ascontiguousarray(x[b].T.astype(np.float32, copy=False))
        own_tiles = [c + 4 * k for k in range(NQT)]
        cols = np.concatenate([np.arange(128 * t, 128 * (t + 1)) for t in own_tiles])
        xownT = np.ascontiguousarray(xT[:, cols])
        pos = cols.astype(np.float32)
        posq = np.ascontiguousarray(np.broadcast_to(pos[None, :], (128, NOWN)))
        mq = np.zeros((NQT, 4, 128, 512), np.float32)
        for k in range(NQT):
            pk = own_tiles[k]
            for l in range(4):
                mt = 4 * k + l
                mg = 128 * mt + np.arange(128)[:, None]
                qg = 128 * pk + np.arange(128)[None, :]
                tile = np.where(mg > qg, NEG, 0.0).astype(np.float32)
                mq[k, l] = np.tile(tile, (1, 4))
        mqm = mq.astype(ml_dtypes.bfloat16)
        m = dict(shared)
        m.update(xcov=xT, xown=xownT, posq=posq, mqm=mqm)
        maps.append(m)
    return maps


def kernel(**inputs):
    maps = _stage(inputs)
    nc = build_program()
    res = run_bass_kernel_spmd(nc, maps, core_ids=list(range(8)))
    out = np.zeros((NB, S, D), np.float32)
    for core in range(8):
        b = core // CPB
        c = core % CPB
        own_tiles = [c + 4 * k for k in range(NQT)]
        cols = np.concatenate([np.arange(128 * t, 128 * (t + 1)) for t in own_tiles])
        y = res.results[core]["yout"]
        out[b, cols, :] = y.T
    return out


# version 09 (final)
# speedup vs baseline: 1.0217x; 1.0217x over previous
import numpy as np
import concourse.bass as bass
import concourse.mybir as mybir
from concourse.bass_utils import run_bass_kernel_spmd
from concourse.alu_op_type import AluOpType as OP

F32 = mybir.dt.float32
BF16 = mybir.dt.bfloat16
U16 = mybir.dt.uint16
I32 = mybir.dt.int32
AF = mybir.ActivationFunctionType
AX = mybir.AxisListType.X

S = 4096
D = 4096
NB = 2
CPB = 4
NQT = 8
NOWN = NQT * 128
NKVH = 8
NQH = 32
HD = 128
DC = D // 128
NEG = -1.0e30
EPS = 1e-5
TWO_PI = float(2.0 * np.pi)
PI = float(np.pi)


class Sem:
    def __init__(self, handle):
        self.h = handle
        self.n = 0

    def inc(self, instr, k=1):
        instr.then_inc(self.h, k)
        self.n += k
        return instr

    def wait(self, eng, val=None):
        v = self.n if val is None else val
        if v:
            eng.wait_ge(self.h, v)


def build_program(dbg=(), dbg_s4=False):
    nc = bass.Bass("TRN2", target_bir_lowering=False)

    def din(name, shape, dt=F32):
        return nc.dram_tensor(name, shape, dt, kind="ExternalInput").ap()

    def dsc(name, shape, dt=F32):
        kind = "ExternalOutput" if name in dbg else "Internal"
        return nc.dram_tensor(name, shape, dt, kind=kind).ap()

    xcov = din("xcov", [D, S])
    xown = din("xown", [D, NOWN])
    wq_i = din("wq", [D, D])
    wk_i = din("wk", [D, NKVH * HD])
    wv_i = din("wv", [D, NKVH * HD])
    wo_i = din("wo", [D, D])
    h4_i = din("h4", [4, D, D], U16)
    lut4_i = din("lut4", [D, 16])
    bias_i = din("bias0", [128, DC])
    cqi = din("cqi", [128, NOWN]); sqi = din("sqi", [128, NOWN])
    cki = din("cki", [128, S]); ski = din("ski", [128, S])
    mqm_i = din("mqm", [NQT, 4, 128, 512], BF16)
    lam_re = din("lam_re", [1, 64]); lam_im = din("lam_im", [1, 64])
    p_re = din("p_re", [1, 64]); p_im = din("p_im", [1, 64])
    q_re = din("q_re", [1, 64]); q_im = din("q_im", [1, 64])
    ct_re = din("ct_re", [1, 64]); ct_im = din("ct_im", [1, 64])
    vc_reT = din("vc_reT", [64, 64]); vc_imT = din("vc_imT", [64, 64])
    bp_i = din("bp", [64, 1])
    dd_i = din("dsc_d", [1, 1])
    ls_i = din("lstep", [1, 1])
    ar_i = din("a_r", [D, 1024])
    br_i = din("b_r", [1024, D])
    wr_i = din("wr_r", [D, 64])
    e8_i = din("e8", [64, 1024])

    yout = nc.dram_tensor("yout", [D, NOWN], F32, kind="ExternalOutput").ap()

    xnch = dsc("xnch", [D, S], BF16); xncl = dsc("xncl", [D, S], BF16)
    xnoh = dsc("xnoh", [D, NOWN], BF16); xnol = dsc("xnol", [D, NOWN], BF16)
    kdh = dsc("kdh", [NKVH, HD, S], BF16); kdl = dsc("kdl", [NKVH, HD, S], BF16)
    vdh = dsc("vdh", [NKVH, S, HD], BF16); vdl = dsc("vdl", [NKVH, S, HD], BF16)
    qdh = dsc("qdh", [NKVH, HD, 4, NOWN], BF16)
    qdl = dsc("qdl", [NKVH, HD, 4, NOWN], BF16)
    wqh = dsc("wqh", [D, D], BF16); wql = dsc("wql", [D, D], BF16)
    wkh = dsc("wkh", [D, NKVH * HD], BF16); wkl = dsc("wkl", [D, NKVH * HD], BF16)
    wvh = dsc("wvh", [D, NKVH * HD], BF16); wvl = dsc("wvl", [D, NKVH * HD], BF16)
    woh = dsc("woh", [D, D], BF16); wol = dsc("wol", [D, D], BF16)
    arh = dsc("arh", [D, 1024], BF16); arl = dsc("arl", [D, 1024], BF16)
    brh = dsc("brh", [1024, D], BF16); brl = dsc("brl", [1024, D], BF16)
    wlh = dsc("wlh", [D, D], BF16); wll = dsc("wll", [D, D], BF16)
    wrdh = dsc("wrdh", [D, 64], BF16); wrdl = dsc("wrdl", [D, 64], BF16)
    yffn = dsc("yffn", [D, NOWN])
    aodh = dsc("aodh", [D, NOWN], BF16); aodl = dsc("aodl", [D, NOWN], BF16)
    x1d = dsc("x1d", [D, NOWN])
    x1nh = dsc("x1nh", [D, NOWN], BF16); x1nl = dsc("x1nl", [D, NOWN], BF16)
    kpad = dsc("kpad", [1, 4224])
    dsum = dsc("dsum", [1, S]); dsq = dsc("dsq", [1, S])
    if dbg_s4:
        s4dbg = nc.dram_tensor("s4dbg", [12, 128, 32], F32, kind="ExternalOutput").ap()
        s4dbg2 = nc.dram_tensor("s4dbg2", [4, 128, 64], F32, kind="ExternalOutput").ap()
        s4dbg3 = nc.dram_tensor("s4dbg3", [8, 128, 128], F32, kind="ExternalOutput").ap()

    ctxs = []

    def sb(name, shape, dt=F32):
        t = nc.sbuf_tensor(name, shape, dt)
        h = t.__enter__()
        ctxs.append(t)
        return h

    def psum(name, shape, dt=F32):
        t = nc.psum_tensor(name, shape, dt)
        h = t.__enter__()
        ctxs.append(t)
        return h

    def mksem(name):
        t = nc.semaphore(name)
        h = t.__enter__()
        ctxs.append(t)
        return Sem(h)

    identb = sb("identb", [128, 128], BF16)
    identf = sb("identf", [128, 128], F32)
    revf = sb("revf", [128, 128], F32)
    idint = sb("idint", [128, 128], I32)
    idint2 = sb("idint2", [128, 128], I32)
    ones_col_b = sb("ones_col_b", [128, 1], BF16)
    ones_row_f = sb("ones_row_f", [1, 128], F32)
    ones_col_f = sb("ones_col_f", [128, 1], F32)
    invf = sb("invf", [128, 1])
    pcol = sb("pcol", [128, 1])
    csign = sb("csign", [128, 1])
    cpi = sb("cpi", [128, 1])
    ceps = sb("ceps", [128, 1])
    gxf = sb("gxf", [128, 8 * 512])
    cq = gxf[:, 0:NOWN]; sq = gxf[:, NOWN:2 * NOWN]
    posqs = gxf[:, 2 * NOWN:3 * NOWN]
    sra = sb("sra", [1, 512]); srb = sb("srb", [1, 512])
    xa = sb("xa", [128, 512]); xb = sb("xb", [128, 512])
    tu = sb("tu", [128, 512]); tv = sb("tv", [128, 512])
    thA = sb("thA", [128, 512], BF16); tlA = sb("tlA", [128, 512], BF16)
    thB = sb("thB", [128, 512], BF16); tlB = sb("tlB", [128, 512], BF16)
    xph = sb("xph", [128, DC * 512], BF16)
    xpl = sb("xpl", [128, DC * 512], BF16)
    wph = sb("wph", [128, DC * 128], BF16)
    wpl = sb("wpl", [128, DC * 128], BF16)
    rint = sb("rint", [128, 512], I32)
    bcast_a = sb("bcast_a", [128, 512]); bcast_b = sb("bcast_b", [128, 512])
    tw = bcast_a
    posk = bcast_b
    rsw = bcast_a
    qth = sb("qth", [128, 512], BF16); qtl = sb("qtl", [128, 512], BF16)
    kthA = sb("kthA", [128, 128], BF16); ktlA = sb("ktlA", [128, 128], BF16)
    kthB = sb("kthB", [128, 128], BF16); ktlB = sb("ktlB", [128, 128], BF16)
    vthA = sb("vthA", [128, 128], BF16); vtlA = sb("vtlA", [128, 128], BF16)
    vthB = sb("vthB", [128, 128], BF16); vtlB = sb("vtlB", [128, 128], BF16)
    prf = sb("prf", [128, 512])
    ck = prf
    gt64 = prf[0:64, :]
    prhA = sb("prhA", [128, 512], BF16); prlA = sb("prlA", [128, 512], BF16)
    prhB = sb("prhB", [128, 512], BF16); prlB = sb("prlB", [128, 512], BF16)
    mq4 = sb("mq4", [128, 4 * 512], BF16)
    dnv = sra
    _unused_dnv = None
    dnb = sb("dnb", [128, 512])
    sk = dnb
    bigph = sb("bigph", [128, DC * 128], BF16)
    bigpl = sb("bigpl", [128, DC * 128], BF16)
    aoph, aopl = bigph, bigpl
    bcol = sb("bcol", [128, DC])
    h4t = sb("h4t", [128, 4 * 1024], U16)
    jt16 = sb("jt16", [128, 1024], U16)
    sh16 = sb("sh16", [128, 1024], U16)
    ix16 = sb("ix16", [128, 1024], U16)
    eqt = sb("eqt", [128, 1024], BF16)
    dwh, dwl = bigph, bigpl
    luth = sb("luth", [128, 16])
    lutl = sb("lutl", [128, 16])
    lutf = sb("lutf", [128, 16])
    bdh, bdl = bigph, bigpl
    bdf = sb("bdf", [128, 128])
    hgh = sb("hgh", [128, 8 * 512], BF16)
    hgl = sb("hgl", [128, 8 * 512], BF16)
    e8f = sb("e8f", [64, 1024])
    sc64 = sb("sc64", [128, 64]); sc64b = sb("sc64b", [128, 64])
    msk64 = sb("msk64", [128, 64]); g64 = sb("g64", [128, 64])
    dn64 = sb("dn64", [128, 1])
    m8a = sb("m8a", [128, 8]); m8b = sb("m8b", [128, 8])
    omre = sb("omre", [128, 32]); omim = sb("omim", [128, 32])
    gre = sb("gre", [128, 32]); gim = sb("gim", [128, 32])
    cre = sb("cre", [128, 32]); cim = sb("cim", [128, 32])
    k00r = sb("k00r", [128, 32]); k00i = sb("k00i", [128, 32])
    k01r = sb("k01r", [128, 32]); k01i = sb("k01i", [128, 32])
    k10r = sb("k10r", [128, 32]); k10i = sb("k10i", [128, 32])
    k11r = sb("k11r", [128, 32]); k11i = sb("k11i", [128, 32])
    arr = sb("arr", [128, 32]); ari = sb("ari", [128, 32]); ariN = sb("ariN", [128, 32])
    t32a = sb("t32a", [128, 32]); t32b = sb("t32b", [128, 32]); t32c = sb("t32c", [128, 32])
    t32d = sb("t32d", [128, 32])
    lrb = sb("lrb", [128, 64]); lib = sb("lib", [128, 64])
    n1r = sb("n1r", [128, 64]); n1i = sb("n1i", [128, 64])
    n2r = sb("n2r", [128, 64]); n2i = sb("n2i", [128, 64])
    n3r = sb("n3r", [128, 64]); n3i = sb("n3i", [128, 64])
    n4r = sb("n4r", [128, 64]); n4i = sb("n4i", [128, 64])
    t64a = sb("t64a", [128, 64]); t64b = sb("t64b", [128, 64]); t64c = sb("t64c", [128, 64])
    t64d = sb("t64d", [128, 64]); t64e = sb("t64e", [128, 64])
    r1 = sb("r1", [1, 64]); r2 = sb("r2", [1, 64]); r3 = sb("r3", [1, 64]); r4 = sb("r4", [1, 64])
    r5 = sb("r5", [1, 64]); r6 = sb("r6", [1, 64]); r7 = sb("r7", [1, 64]); r8 = sb("r8", [1, 64])
    tr9 = sb("tr9", [1, 64])
    sm64 = [sb(f"sm64_{i}", [1, 64]) for i in range(8)]
    brea = sb("brea", [1, 64]); bima = sb("bima", [1, 64])
    wc128 = sb("wc128", [128, 128]); ws128 = sb("ws128", [128, 128])
    twc = sb("twc", [128, 32]); tws = sb("tws", [128, 32])
    c32s = sb("c32s", [32, 32]); s32sn = sb("s32sn", [32, 32])
    yre = sb("yre", [128, 32]); yim = sb("yim", [128, 32])
    yreT = sb("yreT", [32, 128]); yimT = sb("yimT", [32, 128])
    ktap = sb("ktap", [32, 128])
    stepc = sb("stepc", [128, 1]); dcol = sb("dcol", [128, 1])
    zrow = sb("zrow", [1, 128])
    s4v = sb("s4v", [64, 64]); s4w = sb("s4w", [64, 64]); s4p = sb("s4p", [64, 1])

    ps0 = psum("ps0", [128, 512]); ps1 = psum("ps1", [128, 512])
    ps2 = psum("ps2", [128, 512]); ps3 = psum("ps3", [128, 512])
    ps4 = psum("ps4", [128, 512]); ps5 = psum("ps5", [128, 512])
    pbt = psum("pbt", [128, 128], BF16)
    pft = psum("pft", [128, 128], F32)

    sd = mksem("sd")
    sd2 = mksem("sd2")
    sdA = mksem("sdA")
    sdB = mksem("sdB")
    sdq = mksem("sdq")
    sdm = mksem("sdm")
    sdl = mksem("sdl")
    s2A = mksem("s2A")
    s2B = mksem("s2B")
    sdp = mksem("sdp")
    sdw = mksem("sdw")
    sv = mksem("sv")
    stm = mksem("stm")
    ssm = mksem("ssm")
    sgm = mksem("sgm")
    sout = mksem("sout")

    TE, VE, SE, GE, SY = nc.tensor, nc.vector, nc.scalar, nc.gpsimd, nc.sync

    def mark(eng, s):
        eng.drain()
        eng.sem_inc(s.h, 1)
        s.n += 1
        return s.n

    def barrier():
        mark(TE, stm); mark(VE, sv); mark(SE, ssm); mark(GE, sgm)
        for eng in (TE, VE, SE, GE, SY):
            stm.wait(eng); sv.wait(eng); ssm.wait(eng); sgm.wait(eng)
            sd.wait(eng); sd2.wait(eng); sdA.wait(eng); sdB.wait(eng)
            sdq.wait(eng); sdm.wait(eng); sdl.wait(eng)
            s2A.wait(eng); s2B.wait(eng); sdp.wait(eng); sdw.wait(eng)

    def vtt(out, a, b, op):
        VE.tensor_tensor(out, a, b, op)
        VE.drain()

    def vts(out, a, s1, s2, op1, op2=...):
        if op2 is ...:
            VE.tensor_scalar(out, a, s1, None, op1)
        else:
            VE.tensor_scalar(out, a, s1, s2, op1, op2)
        VE.drain()

    def vstt(out, in0, scal, in1, op0, op1):
        VE.scalar_tensor_tensor(out, in0, scal, in1, op0, op1)
        VE.drain()

    def vcp(out, a):
        VE.tensor_copy(out, a)
        VE.drain()

    def sincol(dst, npart=128):
        SE.activation(dst, dst, AF.Sin, bias=0.0, scale=1.0)

    def rrange(buf, ncols, npart=128):
        for c0 in range(0, ncols, 512):
            w = min(512, ncols - c0)
            ch = buf[0:npart, c0:c0 + w]
            q_ = tu[0:npart, 0:w]
            f_ = xa[0:npart, 0:w]
            vts(q_, ch, float(1.0 / TWO_PI), 0.5, OP.mult, OP.add)
            vcp(rint[0:npart, 0:w], q_)
            vcp(f_, rint[0:npart, 0:w])
            vtt(q_, f_, q_, OP.is_gt)
            vtt(f_, f_, q_, OP.subtract)
            vstt(ch, f_, -TWO_PI, ch, OP.mult, OP.add)
            vts(q_, ch, PI, None, OP.is_gt)
            vstt(ch, q_, -TWO_PI, ch, OP.mult, OP.add)
            vts(q_, ch, -PI, None, OP.is_lt)
            vstt(ch, q_, TWO_PI, ch, OP.mult, OP.add)

    with nc.Block():
        GE.iota(idint[:], [[1, 128]], base=0, channel_multiplier=-1)
        GE.iota(idint2[:], [[1, 128]], base=0, channel_multiplier=1)
        GE.iota(pcol[:], [[1, 1]], base=0, channel_multiplier=1,
                allow_small_or_imprecise_dtypes=True)
        GE.iota(jt16[:].rearrange("p (h j) -> p h j", j=16), [[0, 64], [1, 16]],
                base=0, channel_multiplier=0, allow_small_or_imprecise_dtypes=True)
        GE.memset(ones_col_b[:], 1.0)
        GE.memset(ones_row_f[:], 1.0)
        GE.memset(ones_col_f[:], 1.0)
        GE.memset(cpi[:], PI)
        GE.memset(ceps[:], EPS)
        GE.memset(csign[0:64, :], -1.0)
        GE.memset(csign[64:128, :], 1.0)
        GE.memset(zrow[:], 0.0)
        mark(GE, sgm)
        sgm.wait(VE)
        vts(identb[:], idint[:], 0, None, OP.is_equal)
        vts(identf[:], idint[:], 0, None, OP.is_equal)
        vts(revf[:], idint2[:], 127.0, None, OP.is_equal)
        vts(invf[:], pcol[:], 64.0, None, OP.is_ge)
        vstt(invf[:], invf[:], -64.0, pcol[:], OP.mult, OP.add)
        mark(VE, sv)
        sv.wait(SE)
        SE.activation(invf[:], invf[:], AF.Exp, bias=0.0,
                      scale=float(-np.log(10000.0) / 64.0))
        mark(SE, ssm)
        sd.inc(SY.dma_start(cq[:], cqi), 16)
        sd.inc(SY.dma_start(sq[:], sqi), 16)
        sd.inc(SY.dma_start(bcol[:], bias_i), 16)
        sd.inc(SY.dma_start(e8f[:], e8_i), 16)
        barrier()

        consumed = {0: 0, 1: 0}
        stored = {0: 0, 1: 0}

        def stream_split(src, dsth, dstl, rows, cols):
            for r in range(rows // 128):
                for q in range(cols // 512):
                    i = r * (cols // 512) + q
                    bi = i % 2
                    buf = xa if bi == 0 else xb
                    th = thA if bi == 0 else thB
                    tl = tlA if bi == 0 else tlB
                    sdx = sdA if bi == 0 else sdB
                    s2x = s2A if bi == 0 else s2B
                    sv.wait(SY, consumed[bi])
                    sdx.inc(SY.dma_start(buf[:], src[128 * r:128 * (r + 1), 512 * q:512 * (q + 1)]), 16)
                    sdx.wait(VE)
                    s2x.wait(VE, stored[bi])
                    vcp(th[:], buf[:])
                    vtt(tl[:], buf[:], th[:], OP.subtract)
                    consumed[bi] = mark(VE, sv)
                    sv.wait(SE)
                    s2x.inc(SE.dma_start(dsth[128 * r:128 * (r + 1), 512 * q:512 * (q + 1)], th[:]), 16)
                    s2x.inc(SE.dma_start(dstl[128 * r:128 * (r + 1), 512 * q:512 * (q + 1)], tl[:]), 16)
                    stored[bi] = s2x.n

        stream_split(wq_i, wqh, wql, D, D)
        stream_split(wk_i, wkh, wkl, D, NKVH * HD)
        stream_split(wv_i, wvh, wvl, D, NKVH * HD)
        stream_split(wo_i, woh, wol, D, D)
        stream_split(ar_i, arh, arl, D, 1024)
        stream_split(br_i, brh, brl, 1024, D)
        for ic in range(DC):
            bi = ic % 2
            buf = xa if bi == 0 else xb
            th = thA if bi == 0 else thB
            tl = tlA if bi == 0 else tlB
            sdx = sdA if bi == 0 else sdB
            s2x = s2A if bi == 0 else s2B
            sv.wait(SY, consumed[bi])
            sdx.inc(SY.dma_start(buf[:, 0:64], wr_i[128 * ic:128 * (ic + 1), :]), 16)
            sdx.wait(VE)
            s2x.wait(VE, stored[bi])
            vcp(th[:, 0:64], buf[:, 0:64])
            vtt(tl[:, 0:64], buf[:, 0:64], th[:, 0:64], OP.subtract)
            consumed[bi] = mark(VE, sv)
            sv.wait(SE)
            s2x.inc(SE.dma_start(wrdh[128 * ic:128 * (ic + 1), :], th[:, 0:64]), 16)
            s2x.inc(SE.dma_start(wrdl[128 * ic:128 * (ic + 1), :], tl[:, 0:64]), 16)
            stored[bi] = s2x.n
        barrier()

        def stats_sweep(src, ncols, psA, psB, sum_dst, sq_dst):
            se_done = {0: 0, 1: 0}
            te_done = {0: 0, 1: 0}
            for nt in range(ncols // 512):
                for ic in range(DC):
                    bi = ic % 2
                    buf = xa if bi == 0 else xb
                    sqb = tu if bi == 0 else tv
                    sdx = sdA if bi == 0 else sdB
                    stm.wait(SY, te_done[bi])
                    ssm.wait(SY, se_done[bi])
                    sdx.inc(SY.dma_start(buf[:], src[128 * ic:128 * (ic + 1), 512 * nt:512 * (nt + 1)]), 16)
                    sdx.wait(SE)
                    stm.wait(SE, te_done[bi])
                    SE.activation(sqb[:], buf[:], AF.Square)
                    se_done[bi] = mark(SE, ssm)
                    sdx.wait(TE)
                    stm.inc(TE.matmul(psA[0:1, :], ones_col_f[:], buf[:], start=(ic == 0), stop=(ic == DC - 1)))
                    ssm.wait(TE, se_done[bi])
                    stm.inc(TE.matmul(psB[0:1, :], ones_col_f[:], sqb[:], start=(ic == 0), stop=(ic == DC - 1)))
                    te_done[bi] = stm.n
                stm.wait(VE)
                sd2.wait(VE)
                vcp(sra[:], psA[0:1, :])
                vcp(srb[:], psB[0:1, :])
                mark(VE, sv)
                sv.wait(TE)
                sv.wait(SE)
                sd2.inc(SE.dma_start(sum_dst[:, 512 * nt:512 * (nt + 1)], sra[:]), 16)
                sd2.inc(SE.dma_start(sq_dst[:, 512 * nt:512 * (nt + 1)], srb[:]), 16)

        def ln_finish(n):
            for c0 in range(0, n, 512):
                sv.wait(SY)
                sd2.wait(SY)
                sd.inc(SY.dma_start(sra[:], dsum[:, c0:c0 + 512]), 16)
                sd.inc(SY.dma_start(srb[:], dsq[:, c0:c0 + 512]), 16)
                sd.wait(VE)
                sd2.wait(VE)
                c_ = tu[0:1, :]
                vts(sra[:], sra[:], 1.0 / D, None, OP.mult)
                vtt(c_, sra[:], sra[:], OP.mult)
                vstt(srb[:], srb[:], 1.0 / D, c_, OP.mult, OP.subtract)
                mark(VE, sv)
                sv.wait(SE)
                SE.activation(srb[:], srb[:], AF.Sqrt, bias=ceps[0:1, 0:1], scale=1.0)
                mark(SE, ssm)
                ssm.wait(VE)
                VE.reciprocal(srb[:], srb[:])
                VE.drain()
                vtt(sra[:], sra[:], srb[:], OP.mult)
                mark(VE, sv)
                sv.wait(SE)
                sd2.inc(SE.dma_start(dsum[:, c0:c0 + 512], sra[:]), 16)
                sd2.inc(SE.dma_start(dsq[:, c0:c0 + 512], srb[:]), 16)

        def xn_sweep(src, ncols, rstdv, mrv, dsth, dstl):
            consumed2 = {0: 0, 1: 0}
            stored2 = {0: 0, 1: 0}
            for nt in range(ncols // 512):
                sv.wait(SY)
                sd2.wait(SY)
                sd.inc(SY.dma_start(srb[:], rstdv[:, 512 * nt:512 * (nt + 1)]), 16)
                sd.inc(SY.dma_start(sra[:], mrv[:, 512 * nt:512 * (nt + 1)]), 16)
                sv.wait(TE)
                sd.wait(TE)
                stm.inc(TE.matmul(ps4[:, :], ones_row_f[:], srb[:], start=True, stop=True))
                stm.inc(TE.matmul(ps5[:, :], ones_row_f[:], sra[:], start=True, stop=True))
                stm.wait(VE)
                vcp(bcast_a[:], ps4[:, :])
                vcp(bcast_b[:], ps5[:, :])
                mark(VE, sv)
                for ic in range(DC):
                    bi = ic % 2
                    buf = xa if bi == 0 else xb
                    th = thA if bi == 0 else thB
                    tl = tlA if bi == 0 else tlB
                    sdx = sdA if bi == 0 else sdB
                    s2x = s2A if bi == 0 else s2B
                    sv.wait(SY, consumed2[bi])
                    sdx.inc(SY.dma_start(buf[:], src[128 * ic:128 * (ic + 1), 512 * nt:512 * (nt + 1)]), 16)
                    sdx.wait(VE)
                    s2x.wait(VE, stored2[bi])
                    vtt(buf[:], buf[:], bcast_a[:], OP.mult)
                    vtt(buf[:], buf[:], bcast_b[:], OP.subtract)
                    vcp(th[:], buf[:])
                    vtt(tl[:], buf[:], th[:], OP.subtract)
                    consumed2[bi] = mark(VE, sv)
                    sv.wait(SE)
                    s2x.inc(SE.dma_start(dsth[128 * ic:128 * (ic + 1), 512 * nt:512 * (nt + 1)], th[:]), 16)
                    s2x.inc(SE.dma_start(dstl[128 * ic:128 * (ic + 1), 512 * nt:512 * (nt + 1)], tl[:]), 16)
                    stored2[bi] = s2x.n
                sv.wait(TE)

        stats_sweep(xcov, S, ps0, ps1, dsum, dsq)
        ln_finish(S)
        xn_sweep(xcov, S, dsq, dsum, xnch, xncl)
        barrier()
        stats_sweep(xown, NOWN, ps0, ps1, dsum, dsq)
        ln_finish(NOWN)
        xn_sweep(xown, NOWN, dsq, dsum, xnoh, xnol)
        barrier()

        def load_xpair(srch, srcl, col0):
            stm.wait(SY)
            sv.wait(SY)
            ncols = srch.tensor.shape[1]
            for ic4 in range(0, DC, 4):
                for dst, srcd in ((xph, srch), (xpl, srcl)):
                    sap = bass.AP(srcd.tensor, 128 * ic4 * ncols + col0,
                                  [[ncols, 128], [128 * ncols, 4], [1, 512]])
                    sdp.inc(SY.dma_start(
                        dst[:, 512 * ic4:512 * (ic4 + 4)].rearrange("p (c n) -> p c n", c=4),
                        sap), 16)

        def load_wpair(srch, srcl, ocol0, ow=128):
            stm.wait(SY)
            ncols = srch.tensor.shape[1]
            for ic4 in range(0, DC, 4):
                for dst, srcd in ((wph, srch), (wpl, srcl)):
                    sap = bass.AP(srcd.tensor, 128 * ic4 * ncols + ocol0,
                                  [[ncols, 128], [128 * ncols, 4], [1, ow]])
                    sdw.inc(SY.dma_start(
                        dst[:, ow * ic4:ow * (ic4 + 4)].rearrange("p (c n) -> p c n", c=4),
                        sap), 16)

        def mm3(psx, ow=128, nw=512):
            for ic in range(DC):
                lw = wph[:, ow * ic:ow * ic + ow]
                ll = wpl[:, ow * ic:ow * ic + ow]
                rh = xph[:, 512 * ic:512 * ic + nw]
                rl = xpl[:, 512 * ic:512 * ic + nw]
                stm.inc(TE.matmul(psx, lw, rh, start=(ic == 0), stop=False))
                stm.inc(TE.matmul(psx, lw, rl, start=False, stop=False))
                stm.inc(TE.matmul(psx, ll, rh, start=False, stop=(ic == DC - 1)))

        def rope_apply(psx, ctab, stab, col0, qscale=False):
            stm.wait(VE)
            sd2.wait(VE)
            vcp(tu[:], psx)
            mark(VE, sv)
            sv.wait(SE)
            sd2.inc(SE.dma_start(rsw[0:64, :], tu[64:128, :]), 16)
            sd2.inc(SE.dma_start(rsw[64:128, :], tu[0:64, :]), 16)
            sd2.wait(VE)
            vtt(tu[:], tu[:], ctab[:, col0:col0 + 512], OP.mult)
            vtt(rsw[:], rsw[:], stab[:, col0:col0 + 512], OP.mult)
            vtt(tu[:], tu[:], rsw[:], OP.add)
            if qscale:
                vts(tu[:], tu[:], float(1.0 / np.sqrt(HD)), None, OP.mult)
            vcp(thA[:], tu[:])
            vtt(tlA[:], tu[:], thA[:], OP.subtract)
            mark(VE, sv)

        for nt in range(S // 512):
            load_xpair(xnch, xncl, 512 * nt)
            sv.wait(SY)
            sdl.inc(SY.dma_start(ck[:], cki[:, 512 * nt:512 * (nt + 1)]), 16)
            sdl.inc(SY.dma_start(sk[:], ski[:, 512 * nt:512 * (nt + 1)]), 16)
            sdl.wait(VE)
            for kv in range(2):
                srch = wkh if kv == 0 else wvh
                srcl = wkl if kv == 0 else wvl
                for h in range(NKVH):
                    load_wpair(srch, srcl, 128 * h)
                    psx = ps0 if (h % 2 == 0) else ps1
                    sdw.wait(TE)
                    sdp.wait(TE)
                    sv.wait(TE)
                    mm3(psx[:, :])
                    if kv == 0:
                        rope_apply(psx[:, :], ck, sk, 0)
                        sv.wait(SE)
                        sd2.inc(SE.dma_start(kdh[h, :, 512 * nt:512 * (nt + 1)], thA[:]), 16)
                        sd2.inc(SE.dma_start(kdl[h, :, 512 * nt:512 * (nt + 1)], tlA[:]), 16)
                    else:
                        stm.wait(VE)
                        sd2.wait(VE)
                        vcp(tu[:], psx[:, :])
                        vcp(thA[:], tu[:])
                        vtt(tlA[:], tu[:], thA[:], OP.subtract)
                        mark(VE, sv)
                        sv.wait(TE)
                        for j in range(4):
                            for src_t, dst_d in ((thA, vdh), (tlA, vdl)):
                                sd2.wait(TE)
                                stm.inc(TE.transpose(pbt[:, :], src_t[:, 128 * j:128 * (j + 1)], identb[:]))
                                stm.wait(VE)
                                vcp(thB[:, 0:128], pbt[:, :])
                                mark(VE, sv)
                                sv.wait(SE)
                                sd2.inc(SE.dma_start(
                                    dst_d[h, 512 * nt + 128 * j:512 * nt + 128 * (j + 1), :],
                                    thB[:, 0:128]), 16)
                        sd2.wait(VE)
        barrier()

        for nt in range(NOWN // 512):
            load_xpair(xnoh, xnol, 512 * nt)
            sdp.wait(TE)
            for h in range(NQH):
                load_wpair(wqh, wql, 128 * h)
                psx = ps0 if (h % 2 == 0) else ps1
                sdw.wait(TE)
                sv.wait(TE)
                mm3(psx[:, :])
                rope_apply(psx[:, :], cq, sq, 512 * nt, qscale=True)
                sv.wait(SE)
                g, hj = h // 4, h % 4
                sd2.inc(SE.dma_start(qdh[g, :, hj, 512 * nt:512 * (nt + 1)], thA[:]), 16)
                sd2.inc(SE.dma_start(qdl[g, :, hj, 512 * nt:512 * (nt + 1)], tlA[:]), 16)
        barrier()

        for ot in range(DC):
            sv.wait(SY)
            sdl.inc(SY.dma_start(lutf[:], lut4_i[128 * ot:128 * (ot + 1), :]), 16)
            sdl.wait(VE)
            sd2.wait(VE)
            vcp(eqt[:, 0:16], lutf[:])
            vcp(luth[:], eqt[:, 0:16])
            vtt(lutl[:], lutf[:], luth[:], OP.subtract)
            for icb in range(4):
                sv.wait(SY)
                sdx = sdA if icb % 2 == 0 else sdB
                for p in range(4):
                    sdx.inc(SY.dma_start(h4t[:, 1024 * p:1024 * (p + 1)],
                                        h4_i[p, 128 * ot:128 * (ot + 1), 1024 * icb:1024 * (icb + 1)]), 16)
                sdx.wait(VE)
                for p in range(4):
                    VE.tensor_tensor(sh16[:], h4t[:, 1024 * p:1024 * (p + 1)], jt16[:],
                                     OP.logical_shift_right)
                    VE.drain()
                    if p == 0:
                        vts(ix16[:], sh16[:], 1, None, OP.bitwise_and)
                    else:
                        vts(sh16[:], sh16[:], 1, p, OP.bitwise_and, OP.logical_shift_left)
                        vtt(ix16[:], ix16[:], sh16[:], OP.bitwise_or)
                dho = dwh[:, 1024 * icb:1024 * (icb + 1)]
                dlo = dwl[:, 1024 * icb:1024 * (icb + 1)]
                for c in range(16):
                    vts(eqt[:], ix16[:], float(c), None, OP.is_equal)
                    if c == 0:
                        vts(dho, eqt[:], luth[:, 0:1], None, OP.mult)
                        vts(dlo, eqt[:], lutl[:, 0:1], None, OP.mult)
                    else:
                        vstt(dho, eqt[:], luth[:, c:c + 1], dho, OP.mult, OP.add)
                        vstt(dlo, eqt[:], lutl[:, c:c + 1], dlo, OP.mult, OP.add)
                mark(VE, sv)
            sv.wait(SE)
            sd2.inc(SE.dma_start(wlh[128 * ot:128 * (ot + 1), :], dwh[:]), 16)
            sd2.inc(SE.dma_start(wll[128 * ot:128 * (ot + 1), :], dwl[:]), 16)
        barrier()

        for nt in range(NOWN // 512):
            load_xpair(xnoh, xnol, 512 * nt)
            sdp.wait(TE)
            for oc in range(DC):
                stm.wait(SY)
                sdw.inc(SY.dma_start_transpose(
                    wph[:].rearrange("p (m l) -> p m l", l=128),
                    wlh[128 * oc:128 * (oc + 1), :]), 16)
                sdw.inc(SY.dma_start_transpose(
                    wpl[:].rearrange("p (m l) -> p m l", l=128),
                    wll[128 * oc:128 * (oc + 1), :]), 16)
                psx = ps0 if (oc % 2 == 0) else ps1
                sdw.wait(TE)
                sv.wait(TE)
                mm3(psx[:, :])
                stm.wait(VE)
                sd2.wait(VE)
                vts(tu[:], psx[:, :], bcol[:, oc:oc + 1], None, OP.add)
                mark(VE, sv)
                sv.wait(SE)
                sd2.inc(SE.dma_start(yffn[128 * oc:128 * (oc + 1), 512 * nt:512 * (nt + 1)], tu[:]), 16)
        barrier()

        kbufs = ((kthA, ktlA, vthA, vtlA), (kthB, ktlB, vthB, vtlB))
        prbufs = ((prhA, prlA), (prhB, prlB))
        for k in range(NQT):
            E = 4 * (k + 1)
            sv.wait(SY)
            for l in range(4):
                sdm.inc(SY.dma_start(mq4[:, 512 * l:512 * (l + 1)], mqm_i[k, l, :, :]), 16)
            for g in range(NKVH):
                stm.wait(SY)
                sdq.inc(SY.dma_start(qth[:], qdh[g, :, :, 128 * k:128 * (k + 1)]), 16)
                sdq.inc(SY.dma_start(qtl[:], qdl[g, :, :, 128 * k:128 * (k + 1)]), 16)
                te_used = {0: 0, 1: 0}
                ve_done = {0: 0, 1: 0}
                se_done = {0: 0, 1: 0}
                for j in range(E):
                    bi = j % 2
                    kth, ktl, vth_t, vtl_t = kbufs[bi]
                    prh, prl = prbufs[bi]
                    sdx = sdA if bi == 0 else sdB
                    stm.wait(SY, te_used[bi])
                    sdx.inc(SY.dma_start(kth[:], kdh[g, :, 128 * j:128 * (j + 1)]), 16)
                    sdx.inc(SY.dma_start(ktl[:], kdl[g, :, 128 * j:128 * (j + 1)]), 16)
                    sdx.inc(SY.dma_start(vth_t[:], vdh[g, 128 * j:128 * (j + 1), :]), 16)
                    sdx.inc(SY.dma_start(vtl_t[:], vdl[g, 128 * j:128 * (j + 1), :]), 16)
                    psc = ps0 if bi == 0 else ps1
                    sdx.wait(TE)
                    sdq.wait(TE)
                    ssm.wait(TE, se_done[bi])
                    stm.inc(TE.matmul(psc[:, :], kth[:], qth[:], start=True, stop=False))
                    stm.inc(TE.matmul(psc[:, :], kth[:], qtl[:], start=False, stop=False))
                    stm.inc(TE.matmul(psc[:, :], ktl[:], qth[:], start=False, stop=True))
                    scores_done = stm.n
                    if j >= 4 * k:
                        l = j - 4 * k
                        stm.wait(VE, scores_done)
                        sdm.wait(VE)
                        ssm.wait(VE, se_done[bi])
                        vtt(psc[:, :], psc[:, :], mq4[:, 512 * l:512 * (l + 1)], OP.add)
                        mvd = mark(VE, sv)
                        sv.wait(SE, mvd)
                    else:
                        stm.wait(SE, scores_done)
                    sv.wait(SE, max(ve_done[0], ve_done[1]))
                    SE.activation(prf[:], psc[:, :], AF.Exp)
                    se_done[bi] = mark(SE, ssm)
                    ssm.wait(VE)
                    vcp(prh[:], prf[:])
                    vtt(prl[:], prf[:], prh[:], OP.subtract)
                    ve_done[bi] = mark(VE, sv)
                    sv.wait(TE, ve_done[bi])
                    stm.inc(TE.matmul(ps3[0:1, :], ones_col_b[:], prh[:], start=(j == 0), stop=False))
                    stm.inc(TE.matmul(ps3[0:1, :], ones_col_b[:], prl[:], start=False, stop=(j == E - 1)))
                    stm.inc(TE.matmul(ps2[:, :], vth_t[:], prh[:], start=(j == 0), stop=False))
                    stm.inc(TE.matmul(ps2[:, :], vth_t[:], prl[:], start=False, stop=False))
                    stm.inc(TE.matmul(ps2[:, :], vtl_t[:], prh[:], start=False, stop=(j == E - 1)))
                    te_used[bi] = stm.n
                stm.wait(VE)
                vcp(dnv[:], ps3[0:1, :])
                VE.reciprocal(dnv[:], dnv[:])
                mark(VE, sv)
                sv.wait(TE)
                stm.inc(TE.matmul(ps4[:, :], ones_row_f[:], dnv[:], start=True, stop=True))
                stm.wait(VE)
                vcp(dnb[:], ps4[:, :])
                vtt(tu[:], ps2[:, :], dnb[:], OP.mult)
                for hj in range(4):
                    f = 4 * g + hj
                    vcp(aoph[:, 128 * f:128 * (f + 1)], tu[:, 128 * hj:128 * (hj + 1)])
                    vtt(aopl[:, 128 * f:128 * (f + 1)], tu[:, 128 * hj:128 * (hj + 1)],
                        aoph[:, 128 * f:128 * (f + 1)], OP.subtract)
                mark(VE, sv)
                sv.wait(TE)
            sv.wait(SE)
            sd2.wait(SE)
            for part, dst in ((aoph, aodh), (aopl, aodl)):
                dst_ap = bass.AP(dst.tensor, 128 * k, [[NOWN, 128], [128 * NOWN, DC], [1, 128]])
                sd2.inc(SE.dma_start(dst_ap, part[:].rearrange("d (f q) -> d f q", q=128)), 16)
            sd2.wait(VE)
        barrier()

        for nt in range(NOWN // 512):
            load_xpair(aodh, aodl, 512 * nt)
            sdp.wait(TE)
            for oc in range(DC):
                load_wpair(woh, wol, 128 * oc)
                psx = ps0 if (oc % 2 == 0) else ps1
                sdw.wait(TE)
                sv.wait(TE)
                ssm.wait(TE)
                mm3(psx[:, :])
                mmdone = stm.n
                sv.wait(SY)
                sdq.inc(SY.dma_start(xa[:], xown[128 * oc:128 * (oc + 1), 512 * nt:512 * (nt + 1)]), 16)
                sdm.inc(SY.dma_start(xb[:], yffn[128 * oc:128 * (oc + 1), 512 * nt:512 * (nt + 1)]), 16)
                stm.wait(VE, mmdone)
                sdq.wait(VE)
                sdm.wait(VE)
                sd2.wait(VE)
                ssm.wait(VE)
                vtt(tu[:], psx[:, :], xa[:], OP.add)
                vtt(tu[:], tu[:], xb[:], OP.add)
                mark(VE, sv)
                sv.wait(SE)
                sd2.inc(SE.dma_start(x1d[128 * oc:128 * (oc + 1), 512 * nt:512 * (nt + 1)], tu[:]), 16)
                SE.activation(tv[:], tu[:], AF.Square)
                mark(SE, ssm)
                sv.wait(TE)
                stm.inc(TE.matmul(ps4[0:1, :], ones_col_f[:], tu[:], start=(oc == 0), stop=(oc == DC - 1)))
                ssm.wait(TE)
                stm.inc(TE.matmul(ps5[0:1, :], ones_col_f[:], tv[:], start=(oc == 0), stop=(oc == DC - 1)))
            stm.wait(VE)
            sd2.wait(VE)
            vcp(sra[:], ps4[0:1, :])
            vcp(srb[:], ps5[0:1, :])
            mark(VE, sv)
            sv.wait(TE)
            sv.wait(SE)
            sd2.inc(SE.dma_start(dsum[:, 512 * nt:512 * (nt + 1)], sra[:]), 16)
            sd2.inc(SE.dma_start(dsq[:, 512 * nt:512 * (nt + 1)], srb[:]), 16)
        ln_finish(NOWN)
        xn_sweep(x1d, NOWN, dsq, dsum, x1nh, x1nl)
        barrier()

        sdl.inc(SY.dma_start(tu[0:1, 0:1], ls_i), 16)
        sdl.inc(SY.dma_start(tu[0:1, 1:2], dd_i), 16)
        sdl.wait(SE)
        SE.activation(tu[0:1, 0:1], tu[0:1, 0:1], AF.Exp)
        mark(SE, ssm)
        ssm.wait(VE)
        VE.reciprocal(tu[0:1, 0:1], tu[0:1, 0:1])
        mark(VE, sv)
        sv.wait(TE)
        stm.inc(TE.matmul(ps4[:, 0:2], ones_row_f[:], tu[0:1, 0:2], start=True, stop=True))
        stm.wait(VE)
        vcp(stepc[:], ps4[:, 0:1])
        vcp(dcol[:], ps4[:, 1:2])
        mark(VE, sv)
        sv.wait(GE)
        GE.iota(omre[:], [[1, 32]], base=0, channel_multiplier=32,
                allow_small_or_imprecise_dtypes=True)
        mark(GE, sgm)
        sgm.wait(VE)
        vts(t32d[:], omre[:], float(-TWO_PI / 4096.0), None, OP.mult)
        vts(omre[:], t32d[:], PI / 2, None, OP.add)
        vcp(omim[:], t32d[:])
        rrange(omre, 32)
        rrange(omim, 32)
        mark(VE, sv)
        sv.wait(SE)
        sincol(omre[:]); sincol(omim[:])
        mark(SE, ssm)
        ssm.wait(VE)
        vts(t32a[:], omre[:], 1.0, None, OP.add)
        vtt(t32b[:], t32a[:], t32a[:], OP.mult)
        vtt(t32c[:], omim[:], omim[:], OP.mult)
        vtt(t32b[:], t32b[:], t32c[:], OP.add)
        VE.reciprocal(t32b[:], t32b[:])
        VE.drain()
        vtt(gre[:], omre[:], omre[:], OP.mult)
        vtt(gre[:], gre[:], t32c[:], OP.add)
        vts(gre[:], gre[:], -1.0, 1.0, OP.mult, OP.add)
        vtt(gre[:], gre[:], t32b[:], OP.mult)
        vts(gre[:], gre[:], stepc[:, 0:1], 2.0, OP.mult, OP.mult)
        vtt(gim[:], omim[:], t32b[:], OP.mult)
        vts(gim[:], gim[:], stepc[:, 0:1], -4.0, OP.mult, OP.mult)
        vtt(cre[:], t32a[:], t32b[:], OP.mult)
        vts(cre[:], cre[:], 2.0, None, OP.mult)
        vtt(cim[:], omim[:], t32b[:], OP.mult)
        vts(cim[:], cim[:], -2.0, None, OP.mult)
        mark(VE, sv)
        sd.inc(SY.dma_start(s4p[:], bp_i), 16)
        sd.inc(SY.dma_start(s4v[:], vc_reT), 16)
        sd.inc(SY.dma_start(s4w[:], vc_imT), 16)
        for i2, src_i in enumerate((ct_re, ct_im, p_re, p_im, q_re, q_im, lam_re, lam_im)):
            sd.inc(SY.dma_start(sm64[i2][:], src_i), 16)
        sd.wait(TE)
        sv.wait(TE)
        stm.inc(TE.matmul(ps4[0:1, 0:64], s4p[:], s4v[:], start=True, stop=True))
        stm.inc(TE.matmul(ps5[0:1, 0:64], s4p[:], s4w[:], start=True, stop=True))
        stm.wait(VE)
        vcp(brea[:], ps4[0:1, 0:64])
        vcp(bima[:], ps5[0:1, 0:64])

        def cmulconj(ar_, ai_, br_, bi_, or_, oi_):
            vtt(or_[:], ar_[:], br_[:], OP.mult)
            vtt(tr9[:], ai_[:], bi_[:], OP.mult)
            vtt(or_[:], or_[:], tr9[:], OP.add)
            vtt(oi_[:], ar_[:], bi_[:], OP.mult)
            vtt(tr9[:], ai_[:], br_[:], OP.mult)
            vtt(oi_[:], oi_[:], tr9[:], OP.subtract)

        sd.wait(VE)
        cmulconj(sm64[0], sm64[1], brea, bima, r1, r2)
        cmulconj(sm64[0], sm64[1], sm64[2], sm64[3], r3, r4)
        cmulconj(sm64[4], sm64[5], brea, bima, r5, r6)
        cmulconj(sm64[4], sm64[5], sm64[2], sm64[3], r7, r8)
        mark(VE, sv)
        sv.wait(TE)
        for i2, (src_row, dst) in enumerate(((r1, n1r), (r2, n1i), (r3, n2r), (r4, n2i),
                                             (r5, n3r), (r6, n3i), (r7, n4r), (r8, n4i),
                                             (sm64[6], lrb), (sm64[7], lib))):
            pp = ps4 if (i2 % 2 == 0) else ps5
            stm.inc(TE.matmul(pp[:, 0:64], ones_row_f[:], src_row[:], start=True, stop=True))
            stm.wait(VE)
            vcp(dst[:], pp[:, 0:64])
            mark(VE, sv)
            sv.wait(TE)
        for b in range(32):
            vts(t64a[:], lrb[:], gre[:, b:b + 1], None, OP.subtract)
            vts(t64b[:], lib[:], gim[:, b:b + 1], None, OP.subtract)
            vtt(t64c[:], t64a[:], t64a[:], OP.mult)
            vtt(t64d[:], t64b[:], t64b[:], OP.mult)
            vtt(t64c[:], t64c[:], t64d[:], OP.add)
            VE.reciprocal(t64c[:], t64c[:])
            VE.drain()
            for (nr, ni, outr, outi) in ((n1r, n1i, k00r, k00i), (n2r, n2i, k01r, k01i),
                                         (n3r, n3i, k10r, k10i), (n4r, n4i, k11r, k11i)):
                vtt(t64d[:], nr[:], t64a[:], OP.mult)
                vtt(t64e[:], ni[:], t64b[:], OP.mult)
                vtt(t64d[:], t64d[:], t64e[:], OP.add)
                vtt(t64d[:], t64d[:], t64c[:], OP.mult)
                VE.tensor_reduce(outr[:, b:b + 1], t64d[:], AX, OP.add, negate=True)
                VE.drain()
                vtt(t64d[:], ni[:], t64a[:], OP.mult)
                vtt(t64e[:], nr[:], t64b[:], OP.mult)
                vtt(t64d[:], t64d[:], t64e[:], OP.subtract)
                vtt(t64d[:], t64d[:], t64c[:], OP.mult)
                VE.tensor_reduce(outi[:, b:b + 1], t64d[:], AX, OP.add, negate=True)
                VE.drain()
        vts(t32a[:], k11r[:], 1.0, None, OP.add)
        vtt(t32b[:], t32a[:], t32a[:], OP.mult)
        vtt(t32c[:], k11i[:], k11i[:], OP.mult)
        vtt(t32b[:], t32b[:], t32c[:], OP.add)
        VE.reciprocal(t32b[:], t32b[:])
        VE.drain()
        vtt(arr[:], k01r[:], k10r[:], OP.mult)
        vtt(t32c[:], k01i[:], k10i[:], OP.mult)
        vtt(arr[:], arr[:], t32c[:], OP.subtract)
        vtt(ari[:], k01r[:], k10i[:], OP.mult)
        vtt(t32c[:], k01i[:], k10r[:], OP.mult)
        vtt(ari[:], ari[:], t32c[:], OP.add)
        vtt(t32c[:], arr[:], t32a[:], OP.mult)
        vtt(t32d[:], ari[:], k11i[:], OP.mult)
        vtt(t32c[:], t32c[:], t32d[:], OP.add)
        vtt(t32c[:], t32c[:], t32b[:], OP.mult)
        vtt(t32d[:], ari[:], t32a[:], OP.mult)
        vtt(ari[:], arr[:], k11i[:], OP.mult)
        vtt(t32d[:], t32d[:], ari[:], OP.subtract)
        vtt(t32d[:], t32d[:], t32b[:], OP.mult)
        vtt(arr[:], k00r[:], t32c[:], OP.subtract)
        vtt(ari[:], k00i[:], t32d[:], OP.subtract)
        vtt(t32a[:], arr[:], cre[:], OP.mult)
        vtt(t32b[:], ari[:], cim[:], OP.mult)
        vtt(t32c[:], arr[:], cim[:], OP.mult)
        vtt(t32d[:], ari[:], cre[:], OP.mult)
        vtt(arr[:], t32a[:], t32b[:], OP.subtract)
        vtt(ari[:], t32c[:], t32d[:], OP.add)
        vts(ariN[:], ari[:], -1.0, None, OP.mult)
        mark(VE, sv)
        if dbg_s4:
            sv.wait(SE)
            for i2, t_ in enumerate((omre, omim, gre, gim, cre, cim, k00r, k00i, k11r, k11i, arr, ari)):
                sd2.inc(SE.dma_start(s4dbg[i2, :, :], t_[:]), 16)
            for i2, t_ in enumerate((lrb, lib, n1r, n1i)):
                sd2.inc(SE.dma_start(s4dbg2[i2, :, :], t_[:]), 16)
            sd2.wait(VE)
        sv.wait(GE)
        GE.iota(tu[:, 0:128], [[1, 128]], base=0, channel_multiplier=0,
                allow_small_or_imprecise_dtypes=True)
        GE.iota(tv[:, 0:32], [[1, 32]], base=0, channel_multiplier=0,
                allow_small_or_imprecise_dtypes=True)
        mark(GE, sgm)
        sgm.wait(VE)
        vts(tw[:, 0:128], tu[:, 0:128], pcol[:, 0:1], float(TWO_PI / 128.0), OP.mult, OP.mult)
        vts(wc128[:], tw[:, 0:128], PI / 2, None, OP.add)
        vcp(ws128[:], tw[:, 0:128])
        rrange(wc128, 128)
        rrange(ws128, 128)
        vts(tw[:, 0:32], tv[:, 0:32], pcol[:, 0:1], float(TWO_PI / 4096.0), OP.mult, OP.mult)
        vts(twc[:], tw[:, 0:32], PI / 2, None, OP.add)
        vcp(tws[:], tw[:, 0:32])
        rrange(twc, 32)
        rrange(tws, 32)
        vts(tw[0:32, 0:32], tv[0:32, 0:32], pcol[0:32, 0:1], float(TWO_PI / 32.0), OP.mult, OP.mult)
        vts(c32s[:], tw[0:32, 0:32], PI / 2, None, OP.add)
        vcp(s32sn[:], tw[0:32, 0:32])
        rrange(c32s, 32, npart=32)
        rrange(s32sn, 32, npart=32)
        mark(VE, sv)
        sv.wait(SE)
        sincol(wc128[:]); sincol(ws128[:]); sincol(twc[:]); sincol(tws[:])
        sincol(c32s[:], 32); sincol(s32sn[:], 32)
        mark(SE, ssm)
        ssm.wait(VE)
        vts(c32s[:], c32s[:], float(1.0 / 4096.0), None, OP.mult)
        vts(s32sn[:], s32sn[:], float(-1.0 / 4096.0), None, OP.mult)
        mark(VE, sv)
        sv.wait(TE)
        stm.inc(TE.matmul(ps4[:, 0:32], wc128[:], arr[:], start=True, stop=False))
        stm.inc(TE.matmul(ps4[:, 0:32], ws128[:], ariN[:], start=False, stop=True))
        stm.inc(TE.matmul(ps5[:, 0:32], ws128[:], arr[:], start=True, stop=False))
        stm.inc(TE.matmul(ps5[:, 0:32], wc128[:], ari[:], start=False, stop=True))
        stm.wait(VE)
        vcp(yre[:], ps4[:, 0:32])
        vcp(yim[:], ps5[:, 0:32])
        vtt(t32a[:], yre[:], twc[:], OP.mult)
        vtt(t32b[:], yim[:], tws[:], OP.mult)
        vtt(t32c[:], yre[:], tws[:], OP.mult)
        vtt(t32d[:], yim[:], twc[:], OP.mult)
        vtt(yre[:], t32a[:], t32b[:], OP.subtract)
        vtt(yim[:], t32c[:], t32d[:], OP.add)
        mark(VE, sv)
        sv.wait(TE)
        stm.inc(TE.transpose(pft[0:32, 0:128], yre[:], identf[:]))
        stm.wait(VE)
        vcp(yreT[:], pft[0:32, 0:128])
        mark(VE, sv)
        sv.wait(TE)
        stm.inc(TE.transpose(pft[0:32, 0:128], yim[:], identf[:]))
        stm.wait(VE)
        vcp(yimT[:], pft[0:32, 0:128])
        mark(VE, sv)
        sv.wait(TE)
        stm.inc(TE.matmul(ps4[0:32, 0:128], c32s[:], yreT[:], start=True, stop=False))
        stm.inc(TE.matmul(ps4[0:32, 0:128], s32sn[:], yimT[:], start=False, stop=True))
        stm.wait(VE)
        vcp(ktap[:], ps4[0:32, 0:128])
        mark(VE, sv)
        if dbg_s4:
            sv.wait(SE)
            sd2.inc(SE.dma_start(s4dbg3[0, :, :], wc128[:]), 16)
            sd2.inc(SE.dma_start(s4dbg3[1, :, :], ws128[:]), 16)
            sd2.inc(SE.dma_start(s4dbg3[2, :, 0:32], twc[:]), 16)
            sd2.inc(SE.dma_start(s4dbg3[3, :, 0:32], tws[:]), 16)
            sd2.inc(SE.dma_start(s4dbg3[4, 0:32, 0:32], c32s[:]), 16)
            sd2.inc(SE.dma_start(s4dbg3[5, 0:32, 0:32], s32sn[:]), 16)
            sd2.inc(SE.dma_start(s4dbg3[6, :, 0:32], yre[:]), 16)
            sd2.inc(SE.dma_start(s4dbg3[7, :, 0:32], yim[:]), 16)
            sd2.wait(VE)
        sv.wait(SE)
        sd2.inc(SE.dma_start(bass.AP(kpad.tensor, 0, [[1, 128]]), zrow[:]), 16)
        sd2.inc(SE.dma_start(bass.AP(kpad.tensor, 128, [[128, 32], [1, 128]]), ktap[:]), 16)
        sd2.wait(SY)
        for d_ in range(32):
            srcap = bass.AP(kpad.tensor, 1 + 128 * d_, [[1, 128], [1, 128]])
            stm.wait(SY)
            sd.inc(SY.dma_start(bdf[:], srcap), 16)
            sd.wait(TE)
            sv.wait(TE)
            stm.inc(TE.matmul(pft[:, 0:128], revf[:], bdf[:], start=True, stop=True))
            stm.wait(VE)
            vcp(bdh[:, 128 * d_:128 * (d_ + 1)], pft[:, 0:128])
            vtt(bdl[:, 128 * d_:128 * (d_ + 1)], pft[:, 0:128],
                bdh[:, 128 * d_:128 * (d_ + 1)], OP.subtract)
            mark(VE, sv)
        barrier()

        for nt in range(NOWN // 512):
            load_xpair(x1nh, x1nl, 512 * nt)
            stm.wait(SY)
            load_wpair(wrdh, wrdl, 0, ow=64)
            sdw.wait(TE)
            sdp.wait(TE)
            sv.wait(TE)
            for ic in range(DC):
                lw = wph[:, 64 * ic:64 * (ic + 1)]
                ll = wpl[:, 64 * ic:64 * (ic + 1)]
                rh = xph[:, 512 * ic:512 * (ic + 1)]
                rl = xpl[:, 512 * ic:512 * (ic + 1)]
                stm.inc(TE.matmul(ps0[0:64, :], lw, rh, start=(ic == 0), stop=False))
                stm.inc(TE.matmul(ps0[0:64, :], lw, rl, start=False, stop=False))
                stm.inc(TE.matmul(ps0[0:64, :], ll, rh, start=False, stop=(ic == DC - 1)))
            stm.wait(VE)
            vcp(tu[0:64, :], ps0[0:64, :])
            mark(VE, sv)
            sv.wait(TE)
            for jt in range(4):
                stm.inc(TE.transpose(pft[0:128, 0:64], tu[0:64, 128 * jt:128 * (jt + 1)], identf[0:64, 0:64]))
                stm.wait(VE)
                vcp(sc64[:], pft[:, 0:64])
                VE.max(m8a[:], sc64[:])
                VE.drain()
                vcp(sc64b[:], sc64[:])
                VE.match_replace(sc64b[:], m8a[:], sc64b[:], NEG)
                VE.drain()
                VE.max(m8b[:], sc64b[:])
                VE.drain()
                vts(msk64[:], sc64[:], m8b[:, 7:8], None, OP.is_ge)
                mark(VE, sv)
                sv.wait(SE)
                SE.activation(g64[:], sc64[:], AF.Exp)
                mark(SE, ssm)
                ssm.wait(VE)
                vtt(g64[:], g64[:], msk64[:], OP.mult)
                VE.tensor_reduce(dn64[:], g64[:], AX, OP.add)
                VE.drain()
                VE.reciprocal(dn64[:], dn64[:])
                VE.drain()
                vts(g64[:], g64[:], dn64[:, 0:1], 2.0, OP.mult, OP.mult)
                mark(VE, sv)
                sv.wait(TE)
                stm.inc(TE.transpose(pft[0:64, 0:128], g64[:], identf[:]))
                stm.wait(VE)
                vcp(gt64[:, 128 * jt:128 * (jt + 1)], pft[0:64, 0:128])
                mark(VE, sv)
                sv.wait(TE)
            for t_ in range(8):
                stm.inc(TE.matmul(ps1[:, :], e8f[:, 128 * t_:128 * (t_ + 1)], gt64[:], start=True, stop=True))
                stm.wait(VE)
                vcp(gxf[:, 512 * t_:512 * (t_ + 1)], ps1[:, :])
                mark(VE, sv)
                sv.wait(TE)
            for t_ in range(8):
                psx = ps2 if (t_ % 2 == 0) else ps3
                sv.wait(TE)
                stm.wait(SY)
                for ic in range(DC):
                    sdw.inc(SY.dma_start(wph[:, 128 * ic:128 * (ic + 1)],
                                         arh[128 * ic:128 * (ic + 1), 128 * t_:128 * (t_ + 1)]), 16)
                    sdw.inc(SY.dma_start(wpl[:, 128 * ic:128 * (ic + 1)],
                                         arl[128 * ic:128 * (ic + 1), 128 * t_:128 * (t_ + 1)]), 16)
                sdw.wait(TE)
                mm3(psx[:, :])
                stm.wait(VE)
                sout.wait(VE)
                vtt(tv[:], psx[:, :], gxf[:, 512 * t_:512 * (t_ + 1)], OP.mult)
                vcp(hgh[:, 512 * t_:512 * (t_ + 1)], tv[:])
                vtt(hgl[:, 512 * t_:512 * (t_ + 1)], tv[:], hgh[:, 512 * t_:512 * (t_ + 1)], OP.subtract)
                mark(VE, sv)
            for jc in range(DC):
                psx = ps4 if (jc % 2 == 0) else ps5
                sv.wait(TE)
                first = True
                for i_ in range(jc + 1):
                    d_ = jc - i_
                    lwh = bdh[:, 128 * d_:128 * (d_ + 1)]
                    lwl = bdl[:, 128 * d_:128 * (d_ + 1)]
                    rh = xph[:, 512 * i_:512 * (i_ + 1)]
                    rl = xpl[:, 512 * i_:512 * (i_ + 1)]
                    stm.inc(TE.matmul(psx[:, :], lwh, rh, start=first, stop=False))
                    stm.inc(TE.matmul(psx[:, :], lwh, rl, start=False, stop=False))
                    stm.inc(TE.matmul(psx[:, :], lwl, rh, start=False, stop=False))
                    first = False
                for t_ in range(8):
                    w0 = 128 * (t_ % 2)
                    stm.wait(SY)
                    sdw.inc(SY.dma_start(wph[:, w0:w0 + 128],
                                         brh[128 * t_:128 * (t_ + 1), 128 * jc:128 * (jc + 1)]), 16)
                    sdw.inc(SY.dma_start(wpl[:, w0:w0 + 128],
                                         brl[128 * t_:128 * (t_ + 1), 128 * jc:128 * (jc + 1)]), 16)
                    sdw.wait(TE)
                    hh = hgh[:, 512 * t_:512 * (t_ + 1)]
                    hl = hgl[:, 512 * t_:512 * (t_ + 1)]
                    last = (t_ == 7)
                    stm.inc(TE.matmul(psx[:, :], wph[:, w0:w0 + 128], hh, start=False, stop=False))
                    stm.inc(TE.matmul(psx[:, :], wph[:, w0:w0 + 128], hl, start=False, stop=False))
                    stm.inc(TE.matmul(psx[:, :], wpl[:, w0:w0 + 128], hh, start=False, stop=last))
                mmdone = stm.n
                sv.wait(SY)
                sdm.inc(SY.dma_start(xa[:], x1d[128 * jc:128 * (jc + 1), 512 * nt:512 * (nt + 1)]), 16)
                stm.wait(VE, mmdone)
                sdm.wait(VE)
                sout.wait(VE)
                VE.scalar_tensor_tensor(tv[:], xph[:, 512 * jc:512 * (jc + 1)], dcol[:, 0:1],
                                        psx[:, :], OP.mult, OP.add)
                VE.drain()
                VE.scalar_tensor_tensor(tv[:], xpl[:, 512 * jc:512 * (jc + 1)], dcol[:, 0:1],
                                        tv[:], OP.mult, OP.add)
                VE.drain()
                vtt(tv[:], tv[:], xa[:], OP.add)
                mark(VE, sv)
                sv.wait(SE)
                sout.inc(SE.dma_start(yout[128 * jc:128 * (jc + 1), 512 * nt:512 * (nt + 1)], tv[:]), 16)
        sout.wait(SE)

    for c in reversed(ctxs):
        c.__exit__(None, None, None)
    return nc



def _stage(inputs):
    import ml_dtypes
    x = np.asarray(inputs["L0_attn_x"])
    wq = np.ascontiguousarray(np.asarray(inputs["L0_attn_Wq"], np.float32))
    wk = np.ascontiguousarray(np.asarray(inputs["L0_attn_Wk"], np.float32))
    wv = np.ascontiguousarray(np.asarray(inputs["L0_attn_Wv"], np.float32))
    wo = np.ascontiguousarray(np.asarray(inputs["L0_attn_Wo"], np.float32))
    qw = np.ascontiguousarray(np.asarray(inputs["L0_ffn_qweight"], np.int32))
    h4 = np.ascontiguousarray(qw[0:4]).view("<u2").reshape(4, D, 256)
    h4 = np.ascontiguousarray(np.repeat(h4, 16, axis=2))
    lut4 = np.ascontiguousarray(np.asarray(inputs["L0_ffn_lut4"], np.float32))
    bias0 = np.ascontiguousarray(np.asarray(inputs["L0_ffn_bias"], np.float32).reshape(DC, 128).T)
    lam = np.asarray(inputs["L1_attn_Lambda_ri"], np.float32)
    pri = np.asarray(inputs["L1_attn_p_ri"], np.float32)
    qri = np.asarray(inputs["L1_attn_q_ri"], np.float32)
    vc = np.asarray(inputs["L1_attn_Vc_ri"], np.float32)
    ct = np.asarray(inputs["L1_attn_Ct_ri"], np.float32)
    bp = np.asarray(inputs["L1_attn_Bp"], np.float32)
    dd = np.asarray(inputs["L1_attn_D"], np.float32)
    ls = np.asarray(inputs["L1_attn_log_step"], np.float32)
    A = np.asarray(inputs["L1_ffn_A"], np.float32)
    B = np.asarray(inputs["L1_ffn_B"], np.float32)
    Wr = np.asarray(inputs["L1_ffn_Wr"], np.float32)
    a_r = np.ascontiguousarray(A.transpose(2, 0, 1).reshape(D, 1024))
    b_r = np.ascontiguousarray(B.transpose(0, 2, 1).reshape(1024, D))
    wr_r = np.ascontiguousarray(Wr.T)
    e8 = np.zeros((64, 1024), np.float32)
    for k in range(64):
        e8[k, k * 16:(k + 1) * 16] = 1.0

    shared = dict(
        wq=wq, wk=wk, wv=wv, wo=wo, h4=h4, lut4=lut4, bias0=bias0,
        lam_re=np.ascontiguousarray(lam[:, 0]).reshape(1, 64),
        lam_im=np.ascontiguousarray(lam[:, 1]).reshape(1, 64),
        p_re=np.ascontiguousarray(pri[:, 0]).reshape(1, 64),
        p_im=np.ascontiguousarray(pri[:, 1]).reshape(1, 64),
        q_re=np.ascontiguousarray(qri[:, 0]).reshape(1, 64),
        q_im=np.ascontiguousarray(qri[:, 1]).reshape(1, 64),
        ct_re=np.ascontiguousarray(ct[0, :, 0]).reshape(1, 64),
        ct_im=np.ascontiguousarray(ct[0, :, 1]).reshape(1, 64),
        vc_reT=np.ascontiguousarray(vc[:, :, 0].T),
        vc_imT=np.ascontiguousarray(vc[:, :, 1].T),
        bp=np.ascontiguousarray(bp).reshape(64, 1),
        dsc_d=np.ascontiguousarray(dd).reshape(1, 1),
        lstep=np.ascontiguousarray(ls).reshape(1, 1),
        a_r=a_r, b_r=b_r, wr_r=wr_r, e8=e8)

    import jax
    import jax.numpy as jnp
    with jax.default_device(jax.local_devices(backend="cpu")[0]):
        inv = 1.0 / (10000.0 ** (jnp.arange(0, HD, 2, dtype=jnp.float32)[: HD // 2] / HD))
        fr = jnp.arange(S, dtype=jnp.float32)[:, None] * inv[None, :]
        cosf = np.asarray(jax.device_get(jnp.cos(fr)))
        sinf = np.asarray(jax.device_get(jnp.sin(fr)))
    ckt = np.ascontiguousarray(np.concatenate([cosf.T, cosf.T], 0))
    skt = np.ascontiguousarray(np.concatenate([-sinf.T, sinf.T], 0))
    maps = []
    for core in range(8):
        b = core // CPB
        c = core % CPB
        xT = np.ascontiguousarray(x[b].T.astype(np.float32, copy=False))
        own_tiles = [c + 4 * k for k in range(NQT)]
        cols = np.concatenate([np.arange(128 * t, 128 * (t + 1)) for t in own_tiles])
        xownT = np.ascontiguousarray(xT[:, cols])
        cqt = np.ascontiguousarray(ckt[:, cols])
        sqt = np.ascontiguousarray(skt[:, cols])
        mq = np.zeros((NQT, 4, 128, 512), np.float32)
        for k in range(NQT):
            pk = own_tiles[k]
            for l in range(4):
                mt = 4 * k + l
                mg = 128 * mt + np.arange(128)[:, None]
                qg = 128 * pk + np.arange(128)[None, :]
                tile = np.where(mg > qg, NEG, 0.0).astype(np.float32)
                mq[k, l] = np.tile(tile, (1, 4))
        mqm = mq.astype(ml_dtypes.bfloat16)
        m = dict(shared)
        m.update(xcov=xT, xown=xownT, cqi=cqt, sqi=sqt, cki=ckt, ski=skt, mqm=mqm)
        maps.append(m)
    return maps


def kernel(**inputs):
    maps = _stage(inputs)
    nc = build_program()
    res = run_bass_kernel_spmd(nc, maps, core_ids=list(range(8)))
    out = np.zeros((NB, S, D), np.float32)
    for core in range(8):
        b = core // CPB
        c = core % CPB
        own_tiles = [c + 4 * k for k in range(NQT)]
        cols = np.concatenate([np.arange(128 * t, 128 * (t + 1)) for t in own_tiles])
        y = res.results[core]["yout"]
        out[b, cols, :] = y.T
    return out
